# version 09
# baseline (speedup 1.0000x reference)
import numpy as np
import ml_dtypes
import concourse.bass as bass
import concourse.mybir as mybir
from concourse.bass_utils import run_bass_kernel_spmd

F32 = mybir.dt.float32
BF16 = mybir.dt.bfloat16
AF = mybir.ActivationFunctionType
ALU = mybir.AluOpType

E = 1024
NH = 16
HD = 64
NKK = 8
DHID = 4096
NOC = 32
GRP = 512
TILE = 1024


class Ctr:
    def __init__(self, name, owner, step):
        self.name = name
        self.owner = owner
        self.step = step
        self.sem = None
        self.v = 0
        self.is_dma = step == 16


class Buf:
    def __init__(self):
        self.wmark = None
        self.readers = []


SERIAL = False


class Plan:
    def __init__(self):
        self.steps = []
        self.ctrs = []

    def op(self, eng, fn, waits=(), inc=None):
        ws = []
        if SERIAL:
            for c in self.ctrs:
                if c.v > 0:
                    ws.append((c, c.v))
        for wv in waits:
            if wv is None:
                continue
            c, v = wv
            assert v <= c.v, f"wait on future mark {c.name} {v} > {c.v}"
            if v > 0:
                ws.append((c, v))
        self.steps.append((eng, ws, fn, inc))
        if inc is not None:
            inc.v += inc.step
            return (inc, inc.v)
        return None

    def read(self, eng, fn, bufs_r=(), waits=(), inc=None):
        ws = list(waits)
        for b in bufs_r:
            if b.wmark is not None:
                ws.append(b.wmark)
        m = self.op(eng, fn, ws, inc)
        if m is not None:
            for b in bufs_r:
                b.readers.append(m)
        return m

    def write(self, eng, fn, bufs_w=(), bufs_r=(), waits=(), inc=None):
        ws = list(waits)
        for b in bufs_w:
            ws.extend(b.readers)
            if b.wmark is not None:
                ws.append(b.wmark)
        for b in bufs_r:
            if b.wmark is not None:
                ws.append(b.wmark)
        m = self.op(eng, fn, ws, inc)
        if m is not None:
            for b in bufs_w:
                b.readers = []
                b.wmark = m
            for b in bufs_r:
                b.readers.append(m)
        else:
            for b in bufs_w:
                b.readers = []
                b.wmark = None
        return m


def _round32(x):
    return (x + 31) & ~31


def build_consts():
    cb = np.zeros((128, 288), dtype=ml_dtypes.bfloat16)
    cb[:, 0] = 1.0
    cb[:, 1:65] = 1.0
    eye = np.eye(64, dtype=ml_dtypes.bfloat16)
    cb[0:64, 65:129] = eye
    cb[64:128, 65:129] = eye
    cb[0, 129:257] = 1.0
    cf = np.zeros((128, 8), dtype=np.float32)
    cf[:, 0] = -1.0
    cf[:, 1] = 1e-5
    mk = np.zeros((128, 512), dtype=np.float32)
    for half in range(2):
        for p in range(128):
            mrow = half * 128 + p
            mk[p, half * 256 + mrow:half * 256 + 256] = 0.125
    return cb, cf, mk


def build_program(seq, debug=False):
    nG = seq // GRP
    nT = seq // TILE
    nc = bass.Bass("TRN2", target_bir_lowering=False)

    xT = nc.dram_tensor("xT", [E, seq], F32, kind="ExternalInput").ap()
    wqkv_t = nc.dram_tensor("wqkv_t", [NKK, 128, 3072], F32, kind="ExternalInput").ap()
    wout_t = nc.dram_tensor("wout_t", [NKK, 128, 1024], F32, kind="ExternalInput").ap()
    w1_t = nc.dram_tensor("w1_t", [NKK, 128, DHID], F32, kind="ExternalInput").ap()
    w2_t = nc.dram_tensor("w2_t", [NKK, 128, DHID], F32, kind="ExternalInput").ap()
    h1_t = nc.dram_tensor("h1_t", [NOC, 128, DHID], F32, kind="ExternalInput").ap()
    h2_t = nc.dram_tensor("h2_t", [NOC, 128, DHID], F32, kind="ExternalInput").ap()
    w3_t = nc.dram_tensor("w3_t", [NOC, 128, 1024], F32, kind="ExternalInput").ap()
    s_all_d = nc.dram_tensor("s_all", [128, 136], F32, kind="ExternalInput").ap()
    outT = nc.dram_tensor("outT", [E, seq], F32, kind="ExternalOutput").ap()
    x2scr = nc.dram_tensor("x2scr", [E, seq], F32,
                           kind="ExternalOutput" if debug else "Internal").ap()

    dbg = {}
    if debug:
        for nm, shp in (("dbg_h3", [128, 8192]), ("dbg_hA", [128, 32768]),
                        ("dbg_u1", [128, 32768]), ("dbg_ws", [128, 4096]),
                        ("dbg_z1", [128, 512]), ("dbg_s1o", [128, 512]),
                        ("dbg_hA0", [128, 512]),
                        ("dbg_g1", [128, 32768]), ("dbg_g2", [128, 32768]),
                        ("dbg_f", [128, 4096])):
            dbg[nm] = nc.dram_tensor(nm, shp, F32, kind="ExternalOutput").ap()

    cb_np, cf_np, mk_np = build_consts()
    cb_d = nc.inline_tensor(cb_np, "cb_const").ap()
    cf_d = nc.inline_tensor(cf_np, "cf_const").ap()
    mk_d = nc.inline_tensor(mk_np, "mk_const").ap()

    ARENA = 212096
    arena_base = nc.bump_sbuf(ARENA)[0]
    off = [0]

    def alloc(name, shape, dtype, at=None):
        nbytes = int(np.prod(shape[1:])) * mybir.dt.size(dtype)
        o = off[0] if at is None else at
        assert o + nbytes <= ARENA, (name, o, nbytes)
        t = nc.alloc_sbuf_tensor_at(name, shape, dtype, offset=arena_base + o)
        if at is None:
            off[0] = _round32(o + nbytes)
        return t

    consts_bf = alloc("consts_bf", [128, 288], BF16)
    consts_f = alloc("consts_f", [128, 8], F32)
    mask2 = alloc("mask2", [128, 512], F32)
    s_sb = alloc("s_sb", [128, 136], F32)
    stats_f = alloc("stats_f", [128, 1024], F32)
    stats_b = alloc("stats_b", [128, 2048], BF16)
    mrb_sb = alloc("mrb_sb", [128, 512], F32)
    rb_sb = alloc("rb_sb", [128, 512], F32)
    wslot = [alloc(f"wslot{i}", [128, 4096], BF16) for i in range(2)]

    base0 = off[0]
    xg = alloc("xg", [128, 8, 512], F32)
    bb1 = alloc("bb1", [128, 8, 512], BF16)
    bb2 = alloc("bb2", [128, 8, 512], BF16)
    h1s = alloc("h1s", [128, 8, 512], BF16)
    qTb = alloc("qTb", [128, 8, 512], BF16)
    kTb = alloc("kTb", [128, 8, 512], BF16)
    ktm = alloc("ktm", [128, 4, 1024], BF16)
    vtm = alloc("vtm", [128, 4, 1024], BF16)
    numh = alloc("numh", [128, 8, 512], BF16)
    a_sb = alloc("a_sb", [128, 8, 512], F32)
    attp_t = [alloc(f"attp{i}", [128, 512], BF16) for i in range(2)]
    attp = [attp_t[0], attp_t[1], attp_t[0], attp_t[1]]
    tmp = [alloc(f"tmp{i}", [128, 512], F32) for i in range(4)]
    S_bf = alloc("S_bf", [128, 1024], BF16)
    S_hi = alloc("S_hi", [128, 512], BF16)
    Z_hi = alloc("Z_hi", [128, 16], BF16)
    Z_rep = alloc("Z_rep", [128, 1024], BF16)
    wqkv_sb = alloc("wqkv_sb", [128, 8, 3072], BF16)
    xg2 = alloc("xg2", [128, 8, 512], F32)
    p1_end = off[0]

    off[0] = base0
    hidA = alloc("hidA", [128, 32, 1024], BF16)
    hidB = alloc("hidB", [128, 32, 1024], BF16)
    flex0 = off[0]
    h3s = alloc("h3s", [128, 8, 1024], BF16)
    fsb = alloc("fsb", [128, 8, 512], F32, at=flex0)
    off[0] = _round32(flex0 + 16384)
    x2g_off = off[0]
    x2g = alloc("x2g", [128, 8, 512], F32)
    wslot2 = alloc("wslot2", [128, 4096], BF16, at=x2g_off)
    bbflex0 = off[0]
    bb1b = alloc("bb1b", [128, 8, 512], BF16)
    bb2b = alloc("bb2b", [128, 8, 512], BF16)
    x2g2 = alloc("x2g2", [128, 8, 512], F32, at=bbflex0)
    p2_end = off[0]
    assert max(p1_end, p2_end) <= ARENA, (p1_end, p2_end)

    pb = [nc.alloc_psum_tensor(f"pb{i}", [128, 512], F32) for i in range(8)]

    c_pe = Ctr("c_pe", "pe", 1)
    c_ve = Ctr("c_ve", "ve", 1)
    c_ac = Ctr("c_ac", "ac", 1)
    c_gp = Ctr("c_gp", "gp", 16)
    c_wl = [Ctr("c_wl0", "gp", 16), Ctr("c_wl1", "gp", 16)]
    c_sy = Ctr("c_sy", "sy", 16)
    c_xl = [Ctr("c_xl0", "sy", 16), Ctr("c_xl1", "sy", 16)]
    c_st = Ctr("c_st", "sy", 16)
    c_p2 = [Ctr("c_p2a", "sy", 16), Ctr("c_p2b", "sy", 16)]
    c_out = Ctr("c_out", "sy", 16)

    P = Plan()
    P.ctrs = [c_pe, c_ve, c_ac, c_gp, c_sy, c_st, c_out] + c_wl + c_xl + c_p2

    bXG, bBB1, bBB2, bH1, bQT, bKT, bKTM, bVTM = (Buf() for _ in range(8))
    bNUM, bASB, bSBF, bSHI, bZHI, bZREP, bWQKV, bXG2 = (Buf() for _ in range(8))
    bSTF, bSTB, bMRB, bRB = (Buf() for _ in range(4))
    bHIDA, bHIDB = Buf(), Buf()
    bH3 = Buf()
    bH3g = [Buf(), Buf()]
    bX2G = Buf()
    bFLXB = Buf()
    bATT_t = [Buf(), Buf()]
    bATT = [bATT_t[0], bATT_t[1], bATT_t[0], bATT_t[1]]
    bTMP = [Buf() for _ in range(4)]
    bPB = [Buf() for _ in range(8)]
    bWS = [Buf() for _ in range(2)]

    ones_col = consts_bf[:, 0:1]
    ones_blk = consts_bf[:, 1:65]
    id_lo = consts_bf[0:64, 65:129]
    id_hi = consts_bf[64:128, 65:129]
    ones_row = consts_bf[0:1, 129:257]
    negone = consts_f[:, 0:1]
    eps_col = consts_f[0:1, 1:2]

    P.op("gp", lambda e: e.dma_start(consts_bf[:, :], cb_d), inc=c_gp)
    P.op("gp", lambda e: e.dma_start(consts_f[:, :], cf_d), inc=c_gp)
    P.op("gp", lambda e: e.dma_start(mask2[:, :], mk_d), inc=c_gp)
    P.op("gp", lambda e: e.dma_start(s_sb[:, :], s_all_d), inc=c_gp)
    m_consts = P.write("gp", lambda e: e.dma_start(
        wqkv_sb[:, :, :], wqkv_t.rearrange("k p o -> p k o")),
        bufs_w=[bWQKV], inc=c_gp)

    def accum_mms(make_fn, nacc, bank_buf, src_bufs, extra_waits=()):
        for i in range(nacc):
            fn = make_fn(i)
            if i == 0:
                P.write("pe", fn, bufs_w=[bank_buf], bufs_r=list(src_bufs),
                        waits=list(extra_waits),
                        inc=(c_pe if nacc == 1 else None))
            elif i == nacc - 1:
                P.read("pe", fn, bufs_r=[bank_buf] + list(src_bufs), inc=c_pe)
            else:
                P.read("pe", fn, bufs_r=[bank_buf])
        bank_buf.wmark = (c_pe, c_pe.v)
        bank_buf.readers = []

    def emit_elu(bank_buf, bank_t, dst_buf, dst_view, ti):
        tA, tB = bTMP[2 * ti], bTMP[2 * ti + 1]
        tAa, tBa = tmp[2 * ti], tmp[2 * ti + 1]
        P.write("ac", lambda e: e.activation(tAa[:, :], bank_t[:, 0:512],
                                             AF.Relu, scale=negone),
                bufs_w=[tA], bufs_r=[bank_buf], waits=[m_consts], inc=c_ac)
        P.write("ac", lambda e: e.activation(tAa[:, :], tAa[:, :],
                                             AF.Exp, scale=negone),
                bufs_w=[tA], inc=c_ac)
        P.write("ac", lambda e: e.activation(tBa[:, :], bank_t[:, 0:512], AF.Relu),
                bufs_w=[tB], bufs_r=[bank_buf], inc=c_ac)
        P.write("ve", lambda e: e.tensor_add(dst_view, tAa[:, :], tBa[:, :]),
                bufs_w=[dst_buf], bufs_r=[tA, tB], inc=c_ve)

    def emit_ln(src_buf, src_t, mode, out_buf=None, out_slice=None, dst_buf=None,
                dst_t=None, bb=None, out_bufs=None, store_t0=None):
        if out_bufs is None:
            out_bufs = [out_buf] if out_buf is not None else []
        (b1buf, b1t, b2buf, b2t) = bb
        P.write("ac", lambda e: e.activation(b1t[:, :, :], src_t[:, :, :], AF.Copy),
                bufs_w=[b1buf], bufs_r=[src_buf], inc=c_ac)
        P.write("ac", lambda e: e.activation(b2t[:, :, :], src_t[:, :, :], AF.Square),
                bufs_w=[b2buf], bufs_r=[src_buf], inc=c_ac)
        for bank_buf, bank_t, bsbuf, bst in ((bPB[0], pb[0], b1buf, b1t),
                                             (bPB[1], pb[1], b2buf, b2t)):
            accum_mms(lambda kk, bt=bank_t, bs=bst: (
                lambda e: e.matmul(bt[0:1, 0:512], ones_col, bs[:, kk, :],
                                   start=(kk == 0), stop=(kk == NKK - 1))),
                NKK, bank_buf, [bsbuf], extra_waits=[m_consts])
        mcol = stats_f[0:1, 0:512]
        vcol = stats_f[0:1, 512:1024]
        rcol = mrb_sb[0:1, 0:512]
        P.write("ve", lambda e: e.tensor_scalar_mul(mcol, pb[0][0:1, 0:512], 1.0 / E),
                bufs_w=[bSTF], bufs_r=[bPB[0]], inc=c_ve)
        P.write("ve", lambda e: e.tensor_mul(vcol, mcol, mcol),
                bufs_w=[bSTF], inc=c_ve)
        P.write("ve", lambda e: e.scalar_tensor_tensor(
            vcol, pb[1][0:1, 0:512], 1.0 / E, vcol, ALU.mult, ALU.subtract),
                bufs_w=[bSTF], bufs_r=[bPB[1]], inc=c_ve)
        P.write("ac", lambda e: e.activation(vcol, vcol, AF.Sqrt, bias=eps_col),
                bufs_w=[bSTF], waits=[m_consts], inc=c_ac)
        P.write("ve", lambda e: e.reciprocal(rcol, vcol),
                bufs_w=[bMRB], bufs_r=[bSTF], inc=c_ve)
        P.write("ve", lambda e: e.tensor_mul(mcol, mcol, rcol),
                bufs_w=[bSTF], bufs_r=[bMRB], inc=c_ve)
        P.write("ve", lambda e: e.tensor_copy(stats_b[0:1, 0:512], rcol),
                bufs_w=[bSTB], bufs_r=[bMRB], inc=c_ve)
        P.write("ve", lambda e: e.tensor_copy(vcol, stats_b[0:1, 0:512]),
                bufs_w=[bSTF], inc=c_ve)
        P.write("ve", lambda e: e.tensor_sub(stats_b[0:1, 512:1024], rcol, vcol),
                bufs_w=[bSTB], bufs_r=[bMRB, bSTF], inc=c_ve)
        P.write("ve", lambda e: e.tensor_copy(stats_b[0:1, 1024:1536], mcol),
                bufs_w=[bSTB], bufs_r=[bSTF], inc=c_ve)
        P.write("ve", lambda e: e.tensor_copy(vcol, stats_b[0:1, 1024:1536]),
                bufs_w=[bSTF], inc=c_ve)
        P.write("ve", lambda e: e.tensor_sub(stats_b[0:1, 1536:2048], mcol, vcol),
                bufs_w=[bSTB], bufs_r=[bSTF], inc=c_ve)
        accum_mms(lambda i: (
            lambda e, i=i: e.matmul(pb[0][:, 0:512], ones_row,
                                    stats_b[0:1, 1024 + i * 512:1536 + i * 512],
                                    start=(i == 0), stop=(i == 1))),
            2, bPB[0], [bSTB])
        accum_mms(lambda i: (
            lambda e, i=i: e.matmul(pb[1][:, 0:512], ones_row,
                                    stats_b[0:1, i * 512:512 + i * 512],
                                    start=(i == 0), stop=(i == 1))),
            2, bPB[1], [bSTB])
        P.write("ve", lambda e: e.tensor_copy(mrb_sb[:, :], pb[0][:, 0:512]),
                bufs_w=[bMRB], bufs_r=[bPB[0]], inc=c_ve)
        P.write("ve", lambda e: e.tensor_copy(rb_sb[:, :], pb[1][:, 0:512]),
                bufs_w=[bRB], bufs_r=[bPB[1]], inc=c_ve)
        for cch in range(8):
            P.write("ve", lambda e, c=cch: e.tensor_mul(
                tmp[0][:, :], src_t[:, c, :], rb_sb[:, :]),
                    bufs_w=[bTMP[0]], bufs_r=[src_buf, bRB], inc=c_ve)
            if mode == "h":
                P.write("ve", lambda e, c=cch: e.tensor_sub(
                    out_slice(c), tmp[0][:, :], mrb_sb[:, :]),
                        bufs_w=out_bufs, bufs_r=[bTMP[0], bMRB], inc=c_ve)
            else:
                P.write("ve", lambda e, c=cch: e.tensor_sub(
                    tmp[1][:, :], tmp[0][:, :], mrb_sb[:, :]),
                        bufs_w=[bTMP[1]], bufs_r=[bTMP[0], bMRB], inc=c_ve)
                P.write("ve", lambda e, c=cch: e.tensor_add(
                    dst_t[:, c, :], dst_t[:, c, :], tmp[1][:, :]),
                        bufs_w=[dst_buf], bufs_r=[bTMP[1]], inc=c_ve)


    wslot3 = wslot + [wslot2]
    bWS3 = bWS + [bX2G]

    wl_i = [0]

    def load_wchunk(slot_i, src_ap):
        nk = src_ap.shape[0]
        ctr = c_wl[wl_i[0] % 2]
        wl_i[0] += 1
        return P.write("gp", lambda e, s=wslot3[slot_i], a=src_ap, nk=nk: e.dma_start(
            s[:, :].rearrange("p (k o) -> p k o", k=nk), a.rearrange("k p o -> p k o")),
                       bufs_w=[bWS3[slot_i]], inc=ctr)

    x2st_marks = []
    s_started = [False, False]
    z_started = [False]
    xgs = [xg, xg2]
    bXGs = [bXG, bXG2]

    def load_x(g):
        t0 = g * GRP
        P.write("sy", lambda e, t0=t0, xb=xgs[g % 2]: e.dma_start(
            xb[:, :, :], xT[:, t0:t0 + GRP].rearrange("(c p) t -> p c t", p=128)),
                bufs_w=[bXGs[g % 2]], inc=c_xl[g % 2])

    load_x(0)
    for g in range(nG):
        t0 = g * GRP
        xgc = xgs[g % 2]
        bXGc = bXGs[g % 2]
        emit_ln(bXGc, xgc, "h", out_buf=bH1,
                out_slice=lambda c: h1s[:, c, :], bb=(bBB1, bb1, bBB2, bb2))
        for oc in range(16):
            bi = 2 + (oc % 2)
            accum_mms(lambda kk, bi=bi, oc=oc: (
                lambda e: e.matmul(pb[bi][:, 0:512],
                                   wqkv_sb[:, kk, oc * 128:(oc + 1) * 128],
                                   h1s[:, kk, :],
                                   start=(kk == 0), stop=(kk == NKK - 1))),
                NKK, bPB[bi], [bH1, bWQKV])
            if oc < 8:
                emit_elu(bPB[bi], pb[bi], bQT, qTb[:, oc, :], oc % 2)
            else:
                emit_elu(bPB[bi], pb[bi], bKT, kTb[:, oc - 8, :], oc % 2)
        for mc in range(4):
            for which in ("v", "k"):
                for half in range(2):
                    bi = 4 + half
                    ocol = (2048 if which == "v" else 1024) + half * 512
                    accum_mms(lambda kk, bi=bi, mc=mc, ocol=ocol: (
                        lambda e: e.matmul(pb[bi][:, 0:512],
                                           h1s[:, kk, mc * 128:(mc + 1) * 128],
                                           wqkv_sb[:, kk, ocol:ocol + 512],
                                           start=(kk == 0), stop=(kk == NKK - 1))),
                        NKK, bPB[bi], [bH1, bWQKV])
                    if which == "v":
                        P.write("ac", lambda e, bi=bi, mc=mc, half=half: e.activation(
                            vtm[:, mc, half * 512:(half + 1) * 512],
                            pb[bi][:, 0:512], AF.Copy),
                                bufs_w=[bVTM], bufs_r=[bPB[bi]], inc=c_ac)
                    else:
                        emit_elu(bPB[bi], pb[bi], bKTM,
                                 ktm[:, mc, half * 512:(half + 1) * 512], half)
        if g + 1 < nG:
            load_x(g + 1)
        mwo = [load_wchunk(i, wout_t[:, :, i * 512:(i + 1) * 512]) for i in range(2)]
        for cc in range(2):
            C = 2 * g + cc
            if C > 0:
                P.write("ve", lambda e: e.tensor_scalar_mul(
                    S_bf[0:64, 0:512], pb[6][0:64, 0:512], 0.125),
                        bufs_w=[bSBF], bufs_r=[bPB[6]], inc=c_ve)
                P.write("ve", lambda e: e.tensor_scalar_mul(
                    S_bf[0:64, 512:1024], pb[7][0:64, 0:512], 0.125),
                        bufs_w=[bSBF], bufs_r=[bPB[7]], inc=c_ve)
                P.write("ve", lambda e: e.tensor_scalar_mul(
                    Z_hi[64:128, 0:16], pb[6][64:128, 0:16], 0.125),
                        bufs_w=[bZHI], bufs_r=[bPB[6]], inc=c_ve)
                P.write("ve", lambda e: e.tensor_copy(
                    Z_rep[64:128, :].rearrange("p (h r) -> p h r", h=16),
                    bass.AP(Z_hi, 64 * 16, [[16, 64], [1, 16], [0, 64]])),
                        bufs_w=[bZREP], bufs_r=[bZHI], inc=c_ve)
                srhs = bass.AP(S_bf, 64, [[1024, 64], [128, 8], [1, 64]])
                accum_mms(lambda i, srhs=srhs: (
                    lambda e: e.matmul(pb[1][64:128, 0:512], id_lo, srhs,
                                       start=True, stop=True)),
                    1, bPB[1], [bSBF], extra_waits=[m_consts])
                P.write("ve", lambda e: e.tensor_copy(
                    S_hi[64:128, 0:512], pb[1][64:128, 0:512]),
                        bufs_w=[bSHI], bufs_r=[bPB[1]], inc=c_ve)
                accum_mms(lambda i: (
                    lambda e: e.matmul(pb[0][0:64, 0:16], id_hi,
                                       Z_hi[64:128, 0:16], start=True, stop=True)),
                    1, bPB[0], [bZHI], extra_waits=[m_consts])
                P.write("ve", lambda e: e.tensor_copy(
                    Z_rep[0:64, :].rearrange("p (h r) -> p h r", h=16),
                    bass.AP(pb[0], 0, [[512, 64], [1, 16], [0, 64]])),
                        bufs_w=[bZREP], bufs_r=[bPB[0]], inc=c_ve)
            for q4 in range(4):
                nbi = 2 if q4 % 2 == 0 else 0
                dbi = 3 if q4 % 2 == 0 else 1
                for hh in range(4):
                    h = q4 * 4 + hh
                    hc, hp0, par, qcol = h // 2, (h % 2) * 64, h % 2, hh // 2
                    sbi = 4 + (h % 2)
                    accum_mms(lambda half, sbi=sbi, h=h, hc=hc, hp0=hp0, cc=cc: (
                        lambda e: e.matmul(
                            pb[sbi][:, half * 256:(half + 1) * 256],
                            kTb[hp0:hp0 + 64, hc,
                                cc * 256 + half * 128:cc * 256 + half * 128 + 128],
                            qTb[hp0:hp0 + 64, hc, cc * 256:(cc + 1) * 256],
                            start=(half == 0), stop=(half == 1))),
                        2, bPB[sbi], [bKT, bQT])
                    P.write("ve", lambda e, hh=hh, sbi=sbi: e.tensor_mul(
                        attp[hh][:, :], pb[sbi][:, 0:512], mask2[:, :]),
                            bufs_w=[bATT[hh]], bufs_r=[bPB[sbi]],
                            waits=[m_consts], inc=c_ve)
                    nsrc = [bVTM, bATT[hh]]
                    nacc = 3 if C > 0 else 2

                    def mk_num(i, h=h, hh=hh, cc=cc, par=par, qcol=qcol, C=C,
                               hp0=hp0, hc=hc, nbi=nbi):
                        if i < 2:
                            return (lambda e, half=i: e.matmul(
                                pb[nbi][par * 64:par * 64 + 64,
                                        qcol * 256:qcol * 256 + 256],
                                vtm[:, 2 * cc + half, h * 64:h * 64 + 64],
                                attp[hh][:, half * 256:(half + 1) * 256],
                                start=(hh < 2 and i == 0), stop=False))
                        if par == 0:
                            slhs = S_bf[0:64, h * 64:h * 64 + 64]
                        else:
                            slhs = S_hi[64:128, (h // 2) * 64:(h // 2) * 64 + 64]
                        return (lambda e, slhs=slhs: e.matmul(
                            pb[nbi][par * 64:par * 64 + 64,
                                    qcol * 256:qcol * 256 + 256],
                            slhs, qTb[hp0:hp0 + 64, hc, cc * 256:(cc + 1) * 256],
                            start=False, stop=True))
                    if C > 0:
                        nsrc = nsrc + ([bSBF] if par == 0 else [bSHI]) + [bQT]
                    for i in range(nacc):
                        fn = mk_num(i)
                        if hh < 2 and i == 0:
                            P.write("pe", fn, bufs_w=[bPB[nbi]], bufs_r=nsrc)
                        elif i == 0:
                            P.read("pe", fn, bufs_r=[bPB[nbi]] + nsrc)
                        elif i == nacc - 1:
                            P.read("pe", fn, bufs_r=[bPB[nbi]] + nsrc, inc=c_pe)
                        else:
                            P.read("pe", fn, bufs_r=[bPB[nbi]])
                    bPB[nbi].wmark = (c_pe, c_pe.v)
                    bPB[nbi].readers = []
                    dsrc = [bATT[hh]]
                    if C > 0:
                        dsrc = dsrc + [bZREP, bQT]

                    def mk_den(i, h=h, hh=hh, cc=cc, par=par, qcol=qcol, C=C,
                               hp0=hp0, hc=hc, dbi=dbi):
                        if i < 2:
                            return (lambda e, half=i: e.matmul(
                                pb[dbi][par * 64:par * 64 + 64,
                                        qcol * 256:qcol * 256 + 256],
                                ones_blk, attp[hh][:, half * 256:(half + 1) * 256],
                                start=(hh < 2 and i == 0), stop=False))
                        zlhs = Z_rep[par * 64:par * 64 + 64, h * 64:h * 64 + 64]
                        return (lambda e, zlhs=zlhs: e.matmul(
                            pb[dbi][par * 64:par * 64 + 64,
                                    qcol * 256:qcol * 256 + 256],
                            zlhs, qTb[hp0:hp0 + 64, hc, cc * 256:(cc + 1) * 256],
                            start=False, stop=True))
                    for i in range(nacc):
                        fn = mk_den(i)
                        if hh < 2 and i == 0:
                            P.write("pe", fn, bufs_w=[bPB[dbi]], bufs_r=dsrc,
                                    waits=[m_consts])
                        elif i == 0:
                            P.read("pe", fn, bufs_r=[bPB[dbi]] + dsrc)
                        elif i == nacc - 1:
                            P.read("pe", fn, bufs_r=[bPB[dbi]] + dsrc, inc=c_pe)
                        else:
                            P.read("pe", fn, bufs_r=[bPB[dbi]])
                    bPB[dbi].wmark = (c_pe, c_pe.v)
                    bPB[dbi].readers = []
                ti = q4 % 2
                P.write("ve", lambda e, dbi=dbi, ti=ti: e.tensor_scalar_max(
                    tmp[2 * ti][:, :], pb[dbi][:, 0:512], 1e-6),
                        bufs_w=[bTMP[2 * ti]], bufs_r=[bPB[dbi]], inc=c_ve)
                P.write("ve", lambda e, ti=ti: e.reciprocal(
                    tmp[2 * ti + 1][:, :], tmp[2 * ti][:, :]),
                        bufs_w=[bTMP[2 * ti + 1]], bufs_r=[bTMP[2 * ti]], inc=c_ve)
                P.write("ve", lambda e, q4=q4, cc=cc, nbi=nbi, ti=ti: e.tensor_mul(
                    bass.AP(numh, (2 * q4) * 512 + cc * 256,
                            [[4096, 128], [512, 2], [1, 256]]),
                    bass.AP(pb[nbi], 0, [[512, 128], [256, 2], [1, 256]]),
                    bass.AP(tmp[2 * ti + 1], 0, [[512, 128], [256, 2], [1, 256]])),
                        bufs_w=[bNUM], bufs_r=[bPB[nbi], bTMP[2 * ti + 1]], inc=c_ve)
            upd_srcs = [bKTM, bVTM]
            n_upd = NH * 4
            idx = 0
            for h in range(NH):
                si = 6 if h < 8 else 7
                sbuf = bPB[si]
                for half in range(2):
                    st = not s_started[si - 6]
                    s_started[si - 6] = True
                    last = False
                    fn = (lambda e, si=si, h=h, half=half, cc=cc, st=st:
                          e.matmul(pb[si][0:64, (h % 8) * 64:(h % 8) * 64 + 64],
                                   ktm[:, 2 * cc + half, h * 64:h * 64 + 64],
                                   vtm[:, 2 * cc + half, h * 64:h * 64 + 64],
                                   start=st, stop=False))
                    P.write("pe", fn, bufs_w=[sbuf], bufs_r=[bKTM, bVTM])
                for half in range(2):
                    st = not z_started[0]
                    z_started[0] = True
                    last = h == NH - 1 and half == 1
                    fn = (lambda e, h=h, half=half, cc=cc, st=st:
                          e.matmul(pb[6][64:128, h:h + 1],
                                   ktm[:, 2 * cc + half, h * 64:h * 64 + 64],
                                   ones_col, start=st, stop=False))
                    if last:
                        P.read("pe", fn, bufs_r=[bKTM, bVTM], waits=[m_consts],
                               inc=c_pe)
                    else:
                        P.write("pe", fn, bufs_w=[bPB[6]], bufs_r=[bKTM],
                                waits=[m_consts])
            mupd = (c_pe, c_pe.v)
            for si in (6, 7):
                bPB[si].wmark = mupd
                bPB[si].readers = []
        for oc in range(8):
            bi = 2 + (oc % 4)
            sl = oc // 4
            accum_mms(lambda kk, bi=bi, oc=oc, sl=sl: (
                lambda e: e.matmul(pb[bi][:, 0:512],
                                   bass.AP(wslot3[sl], kk * 512 + (oc % 4) * 128,
                                           [[4096, 128], [1, 128]]),
                                   numh[:, kk, :],
                                   start=(kk == 0), stop=(kk == NKK - 1))),
                NKK, bPB[bi], [bNUM, bWS3[sl]], extra_waits=[mwo[sl]])
            P.write("ac", lambda e, bi=bi, oc=oc: e.activation(
                a_sb[:, oc, :], pb[bi][:, 0:512], AF.Copy),
                    bufs_w=[bASB], bufs_r=[bPB[bi]], inc=c_ac)
        emit_ln(bASB, a_sb, "res", dst_buf=bXGc, dst_t=xgc,
                bb=(bBB1, bb1, bBB2, bb2))
        m = P.read("sy", lambda e, t0=t0, xb=xgc: e.dma_start(
            x2scr[:, t0:t0 + GRP].rearrange("(c p) t -> p c t", p=128), xb[:, :, :]),
                   bufs_r=[bXGc], inc=c_st)
        x2st_marks.append(m)

    wslot3 = wslot + [wslot2]
    bWS3 = bWS + [bX2G]

    wl_i = [0]

    def load_wchunk(slot_i, src_ap):
        nk = src_ap.shape[0]
        ctr = c_wl[wl_i[0] % 2]
        wl_i[0] += 1
        return P.write("gp", lambda e, s=wslot3[slot_i], a=src_ap, nk=nk: e.dma_start(
            s[:, :].rearrange("p (k o) -> p k o", k=nk), a.rearrange("k p o -> p k o")),
                       bufs_w=[bWS3[slot_i]], inc=ctr)

    p2_i = [0]
    m_phase1 = x2st_marks[-1]
    wch_i = [0]

    def next_slot():
        slot = wch_i[0] % 3
        wch_i[0] += 1
        return slot

    for T in range(nT):
        tt0 = T * TILE
        for g2 in range(2):
            t0 = tt0 + g2 * GRP
            ld_waits = [x2st_marks[t0 // GRP]]
            if T == 0 and g2 == 0:
                ld_waits.append(m_phase1)
            P.write("sy", lambda e, t0=t0: e.dma_start(
                x2g[:, :, :], x2scr[:, t0:t0 + GRP].rearrange("(c p) t -> p c t", p=128)),
                    bufs_w=[bX2G], waits=ld_waits, inc=c_p2[p2_i[0] % 2])
            p2_i[0] += 1
            emit_ln(bX2G, x2g, "h", out_bufs=[bH3g[g2], bH3],
                    out_slice=lambda c, g2=g2: bass.AP(
                        h3s, g2 * 512 + c * 1024, [[8192, 128], [1, 512]]),
                    bb=(bFLXB, bb1b, bFLXB, bb2b))
        if debug and T == 0:
            P.read("gp", lambda e: e.dma_start(
                dbg["dbg_h3"].rearrange("p (k t) -> p k t", k=8), h3s[:, :, :]),
                   bufs_r=[bH3], inc=c_gp)
        for stage, wt, scol in (("c1", w1_t, 0), ("c2", w2_t, 32)):
            for j in range(8):
                slot = next_slot()
                mw = load_wchunk(slot, wt[:, :, j * 512:(j + 1) * 512])
                for oc2 in range(4):
                    og = j * 4 + oc2
                    for g2 in range(2):
                        bi = 2 + ((2 * oc2 + g2) % 4)
                        accum_mms(lambda kk, bi=bi, slot=slot, oc2=oc2, g2=g2: (
                            lambda e: e.matmul(
                                pb[bi][:, 0:512],
                                bass.AP(wslot3[slot], kk * 512 + oc2 * 128,
                                        [[4096, 128], [1, 128]]),
                                h3s[:, kk, g2 * 512:(g2 + 1) * 512],
                                start=(kk == 0), stop=(kk == NKK - 1))),
                            NKK, bPB[bi], [bH3g[g2], bWS3[slot]], extra_waits=[mw])
                        if stage == "c1":
                            if debug and T == 0 and j == 0 and oc2 == 0 and g2 == 0:
                                P.write("ac", lambda e, bi=bi: e.activation(
                                    tmp[2][:, :], pb[bi][:, 0:512], AF.Copy),
                                        bufs_w=[bTMP[2]], bufs_r=[bPB[bi]], inc=c_ac)
                                P.read("gp", lambda e: e.dma_start(
                                    dbg["dbg_z1"], tmp[2][:, :]),
                                       bufs_r=[bTMP[2]], inc=c_gp)
                                P.read("gp", lambda e: e.dma_start(
                                    dbg["dbg_ws"], wslot[0][:, :]),
                                       bufs_r=[bWS[0]], inc=c_gp)
                                P.write("ac", lambda e, bi=bi: e.activation(
                                    tmp[3][:, :], pb[bi][:, 0:512], AF.Silu,
                                    scale=s_sb[:, 0:1]),
                                        bufs_w=[bTMP[3]], bufs_r=[bPB[bi]], inc=c_ac)
                                P.read("gp", lambda e: e.dma_start(
                                    dbg["dbg_s1o"], tmp[3][:, :]),
                                       bufs_r=[bTMP[3]], inc=c_gp)
                            P.write("ac", lambda e, bi=bi, og=og, g2=g2, scol=scol:
                                    e.activation(
                                hidA[:, og, g2 * 512:(g2 + 1) * 512], pb[bi][:, 0:512],
                                AF.Silu, scale=s_sb[:, scol + og:scol + og + 1]),
                                    bufs_w=[bHIDA], bufs_r=[bPB[bi]],
                                    waits=[m_consts], inc=c_ac)
                            if debug and T == 0 and j == 0 and oc2 == 0 and g2 == 0:
                                P.read("gp", lambda e: e.dma_start(
                                    dbg["dbg_hA0"], hidA[:, 0, 0:512]),
                                       bufs_r=[bHIDA], inc=c_gp)
                        else:
                            P.write("ac", lambda e, bi=bi, og=og, g2=g2, scol=scol:
                                    e.activation(
                                hidB[:, og, g2 * 512:(g2 + 1) * 512], pb[bi][:, 0:512],
                                AF.Copy, scale=s_sb[:, scol + og:scol + og + 1]),
                                    bufs_w=[bHIDB], bufs_r=[bPB[bi]],
                                    waits=[m_consts], inc=c_ac)
                            P.write("ve", lambda e, og=og, g2=g2: e.tensor_mul(
                                hidA[:, og, g2 * 512:(g2 + 1) * 512],
                                hidA[:, og, g2 * 512:(g2 + 1) * 512],
                                hidB[:, og, g2 * 512:(g2 + 1) * 512]),
                                    bufs_w=[bHIDA], bufs_r=[bHIDB], inc=c_ve)
            if debug and T == 0 and stage == "c1":
                P.read("gp", lambda e: e.dma_start(
                    dbg["dbg_u1"].rearrange("p (k t) -> p k t", k=32), hidA[:, :, :]),
                       bufs_r=[bHIDA], inc=c_gp)
        if debug and T == 0:
            P.read("gp", lambda e: e.dma_start(
                dbg["dbg_hA"].rearrange("p (k t) -> p k t", k=32), hidA[:, :, :]),
                   bufs_r=[bHIDA], inc=c_gp)
        if False:
            pass
        for stage, wt, scol, src, srcb, dst, dstb in (
                ("d", h1_t, 64, hidA, bHIDA, hidB, bHIDB),
                ("e", h2_t, 96, hidB, bHIDB, hidA, bHIDA)):
            for ob in range(8):
                for kq in range(4):
                    slot = next_slot()
                    mw = load_wchunk(slot, wt[kq * 8:(kq + 1) * 8, :,
                                               ob * 512:(ob + 1) * 512])
                    for oc2 in range(4):
                        for g2 in range(2):
                            bi = oc2 * 2 + g2
                            for kk in range(8):
                                fn = (lambda e, bi=bi, slot=slot, oc2=oc2, g2=g2,
                                      kk=kk, kq=kq, src=src: e.matmul(
                                          pb[bi][:, 0:512],
                                          bass.AP(wslot3[slot],
                                                  kk * 512 + oc2 * 128,
                                                  [[4096, 128], [1, 128]]),
                                          src[:, kq * 8 + kk,
                                              g2 * 512:(g2 + 1) * 512],
                                          start=(kq == 0 and kk == 0),
                                          stop=(kq == 3 and kk == 7)))
                                if kq == 0 and kk == 0:
                                    P.write("pe", fn, bufs_w=[bPB[bi]],
                                            bufs_r=[srcb, bWS3[slot]], waits=[mw])
                                elif kk == 7:
                                    P.read("pe", fn,
                                           bufs_r=[bPB[bi], srcb, bWS3[slot]],
                                           inc=c_pe)
                                    bPB[bi].wmark = (c_pe, c_pe.v)
                                    bPB[bi].readers = []
                                else:
                                    P.read("pe", fn, bufs_r=[bPB[bi]],
                                           waits=[mw] if kk == 0 else [])
                for oc2 in range(4):
                    for g2 in range(2):
                        bi = oc2 * 2 + g2
                        j = ob * 4 + oc2
                        P.write("ac", lambda e, bi=bi, j=j, g2=g2, dst=dst,
                                scol=scol: e.activation(
                            dst[:, j, g2 * 512:(g2 + 1) * 512], pb[bi][:, 0:512],
                            AF.Gelu, scale=s_sb[:, scol + j:scol + j + 1]),
                                bufs_w=[dstb], bufs_r=[bPB[bi]],
                                waits=[m_consts], inc=c_ac)
        if debug and T == 0:
            P.read("gp", lambda e: e.dma_start(
                dbg["dbg_g1"].rearrange("p (k t) -> p k t", k=32), hidB[:, :, :]),
                   bufs_r=[bHIDB], inc=c_gp)
            P.read("gp", lambda e: e.dma_start(
                dbg["dbg_g2"].rearrange("p (k t) -> p k t", k=32), hidA[:, :, :]),
                   bufs_r=[bHIDA], inc=c_gp)
        for g2 in range(2):
            t0 = tt0 + g2 * GRP
            for ob in range(2):
                for kq in range(4):
                    slot = next_slot()
                    mw = load_wchunk(slot, w3_t[kq * 8:(kq + 1) * 8, :,
                                               ob * 512:(ob + 1) * 512])
                    for oc2 in range(4):
                        bi = 2 + oc2
                        for kk in range(8):
                            fn = (lambda e, bi=bi, slot=slot, oc2=oc2, g2=g2,
                                  kk=kk, kq=kq: e.matmul(
                                      pb[bi][:, 0:512],
                                      bass.AP(wslot3[slot], kk * 512 + oc2 * 128,
                                              [[4096, 128], [1, 128]]),
                                      hidA[:, kq * 8 + kk,
                                           g2 * 512:(g2 + 1) * 512],
                                      start=(kq == 0 and kk == 0),
                                      stop=(kq == 3 and kk == 7)))
                            if kq == 0 and kk == 0:
                                P.write("pe", fn, bufs_w=[bPB[bi]],
                                        bufs_r=[bHIDA, bWS3[slot]], waits=[mw])
                            elif kk == 7:
                                P.read("pe", fn,
                                       bufs_r=[bPB[bi], bHIDA, bWS3[slot]],
                                       inc=c_pe)
                                bPB[bi].wmark = (c_pe, c_pe.v)
                                bPB[bi].readers = []
                            else:
                                P.read("pe", fn, bufs_r=[bPB[bi]],
                                       waits=[mw] if kk == 0 else [])
                for oc2 in range(4):
                    bi = 2 + oc2
                    oc = ob * 4 + oc2
                    P.write("ac", lambda e, bi=bi, oc=oc: e.activation(
                        fsb[:, oc, :], pb[bi][:, 0:512], AF.Copy,
                        scale=s_sb[:, 128 + oc:128 + oc + 1]),
                            bufs_w=[bH3], bufs_r=[bPB[bi]],
                            waits=[m_consts], inc=c_ac)
            if debug and T == 0 and g2 == 0:
                P.read("gp", lambda e: e.dma_start(
                    dbg["dbg_f"].rearrange("p (k t) -> p k t", k=8), fsb[:, :, :]),
                       bufs_r=[bH3], inc=c_gp)
            P.write("ac", lambda e: e.activation(bb1b[:, :, :], fsb[:, :, :], AF.Copy),
                    bufs_w=[bFLXB], bufs_r=[bH3], inc=c_ac)
            P.write("ac", lambda e: e.activation(bb2b[:, :, :], fsb[:, :, :], AF.Square),
                    bufs_w=[bFLXB], bufs_r=[bH3], inc=c_ac)
            for bank_buf, bank_t, bst in ((bPB[0], pb[0], bb1b), (bPB[1], pb[1], bb2b)):
                accum_mms(lambda kk, bt=bank_t, bs=bst: (
                    lambda e: e.matmul(bt[0:1, 0:512], ones_col, bs[:, kk, :],
                                       start=(kk == 0), stop=(kk == NKK - 1))),
                    NKK, bank_buf, [bFLXB], extra_waits=[m_consts])
            mcol = stats_f[0:1, 0:512]
            vcol = stats_f[0:1, 512:1024]
            rcol = mrb_sb[0:1, 0:512]
            P.write("ve", lambda e: e.tensor_scalar_mul(mcol, pb[0][0:1, 0:512], 1.0 / E),
                    bufs_w=[bSTF], bufs_r=[bPB[0]], inc=c_ve)
            P.write("ve", lambda e: e.tensor_mul(vcol, mcol, mcol),
                    bufs_w=[bSTF], inc=c_ve)
            P.write("ve", lambda e: e.scalar_tensor_tensor(
                vcol, pb[1][0:1, 0:512], 1.0 / E, vcol, ALU.mult, ALU.subtract),
                    bufs_w=[bSTF], bufs_r=[bPB[1]], inc=c_ve)
            P.write("ac", lambda e: e.activation(vcol, vcol, AF.Sqrt, bias=eps_col),
                    bufs_w=[bSTF], waits=[m_consts], inc=c_ac)
            P.write("ve", lambda e: e.reciprocal(rcol, vcol),
                    bufs_w=[bMRB], bufs_r=[bSTF], inc=c_ve)
            P.write("ve", lambda e: e.tensor_mul(mcol, mcol, rcol),
                    bufs_w=[bSTF], bufs_r=[bMRB], inc=c_ve)
            P.write("ve", lambda e: e.tensor_copy(stats_b[0:1, 0:512], rcol),
                    bufs_w=[bSTB], bufs_r=[bMRB], inc=c_ve)
            P.write("ve", lambda e: e.tensor_copy(vcol, stats_b[0:1, 0:512]),
                    bufs_w=[bSTF], inc=c_ve)
            P.write("ve", lambda e: e.tensor_sub(stats_b[0:1, 512:1024], rcol, vcol),
                    bufs_w=[bSTB], bufs_r=[bMRB, bSTF], inc=c_ve)
            P.write("ve", lambda e: e.tensor_copy(stats_b[0:1, 1024:1536], mcol),
                    bufs_w=[bSTB], bufs_r=[bSTF], inc=c_ve)
            P.write("ve", lambda e: e.tensor_copy(vcol, stats_b[0:1, 1024:1536]),
                    bufs_w=[bSTF], inc=c_ve)
            P.write("ve", lambda e: e.tensor_sub(stats_b[0:1, 1536:2048], mcol, vcol),
                    bufs_w=[bSTB], bufs_r=[bSTF], inc=c_ve)
            accum_mms(lambda i: (
                lambda e, i=i: e.matmul(pb[0][:, 0:512], ones_row,
                                        stats_b[0:1, 1024 + i * 512:1536 + i * 512],
                                        start=(i == 0), stop=(i == 1))),
                2, bPB[0], [bSTB])
            accum_mms(lambda i: (
                lambda e, i=i: e.matmul(pb[1][:, 0:512], ones_row,
                                        stats_b[0:1, i * 512:512 + i * 512],
                                        start=(i == 0), stop=(i == 1))),
                2, bPB[1], [bSTB])
            P.write("ve", lambda e: e.tensor_copy(mrb_sb[:, :], pb[0][:, 0:512]),
                    bufs_w=[bMRB], bufs_r=[bPB[0]], inc=c_ve)
            P.write("ve", lambda e: e.tensor_copy(rb_sb[:, :], pb[1][:, 0:512]),
                    bufs_w=[bRB], bufs_r=[bPB[1]], inc=c_ve)
            P.write("sy", lambda e, t0=t0: e.dma_start(
                x2g2[:, :, :], x2scr[:, t0:t0 + GRP].rearrange("(c p) t -> p c t", p=128)),
                    bufs_w=[bFLXB], waits=[x2st_marks[t0 // GRP]],
                    inc=c_p2[p2_i[0] % 2])
            p2_i[0] += 1
            for cch in range(8):
                P.write("ve", lambda e, c=cch: e.tensor_mul(
                    tmp[0][:, :], fsb[:, c, :], rb_sb[:, :]),
                        bufs_w=[bTMP[0]], bufs_r=[bH3, bRB], inc=c_ve)
                P.write("ve", lambda e, c=cch: e.tensor_sub(
                    tmp[1][:, :], tmp[0][:, :], mrb_sb[:, :]),
                        bufs_w=[bTMP[1]], bufs_r=[bTMP[0], bMRB], inc=c_ve)
                P.write("ve", lambda e, c=cch: e.tensor_add(
                    x2g2[:, c, :], x2g2[:, c, :], tmp[1][:, :]),
                        bufs_w=[bFLXB], bufs_r=[bTMP[1]], inc=c_ve)
            P.read("sy", lambda e, t0=t0: e.dma_start(
                outT[:, t0:t0 + GRP].rearrange("(c p) t -> p c t", p=128),
                x2g2[:, :, :]),
                   bufs_r=[bFLXB], inc=c_out)

    final_out = (c_out, c_out.v)

    with (
        nc.semaphore("s_pe") as s_pe,
        nc.semaphore("s_ve") as s_ve,
        nc.semaphore("s_ac") as s_ac,
        nc.semaphore("s_gp") as s_gp,
        nc.semaphore("s_sy") as s_sy,
        nc.semaphore("s_wl0") as s_wl0,
        nc.semaphore("s_wl1") as s_wl1,
        nc.semaphore("s_xl0") as s_xl0,
        nc.semaphore("s_xl1") as s_xl1,
        nc.semaphore("s_st") as s_st,
        nc.semaphore("s_p2a") as s_p2a,
        nc.semaphore("s_p2b") as s_p2b,
        nc.semaphore("s_out") as s_out,
        nc.Block() as block,
    ):
        c_pe.sem, c_ve.sem, c_ac.sem, c_gp.sem, c_sy.sem = s_pe, s_ve, s_ac, s_gp, s_sy
        c_wl[0].sem, c_wl[1].sem = s_wl0, s_wl1
        c_xl[0].sem, c_xl[1].sem = s_xl0, s_xl1
        c_st.sem = s_st
        c_p2[0].sem, c_p2[1].sem = s_p2a, s_p2b
        c_out.sem = s_out

        def runner(name, eng):
            lastw = {}
            for (en, waits, fn, inc) in P.steps:
                if en != name:
                    continue
                for (ctr, v) in waits:
                    if ctr.owner == name and not ctr.is_dma:
                        continue
                    key = id(ctr)
                    if lastw.get(key, 0) >= v:
                        continue
                    eng.wait_ge(ctr.sem, v)
                    lastw[key] = v
                if fn is None:
                    continue
                ins = fn(eng)
                if inc is not None and ins is not None:
                    ins.then_inc(inc.sem, inc.step)

        @block.tensor
        def _(t):
            runner("pe", t)

        @block.vector
        def _(v):
            runner("ve", v)

        @block.scalar
        def _(s):
            runner("ac", s)

        @block.gpsimd
        def _(g):
            runner("gp", g)

        @block.sync
        def _(s):
            runner("sy", s)
            s.wait_ge(c_out.sem, final_out[1])

    return nc


def kernel(L0_attn_x, L0_attn_W_qkv, L0_attn_W_out, L0_ffn_w1_w, L0_ffn_w1_s,
           L0_ffn_w2_w, L0_ffn_w2_s, L0_ffn_h1_w, L0_ffn_h1_s, L0_ffn_h2_w,
           L0_ffn_h2_s, L0_ffn_w3_w, L0_ffn_w3_s):
    x = np.asarray(L0_attn_x)
    Bz, seq, _ = x.shape
    xT = np.ascontiguousarray(np.transpose(x, (0, 2, 1)).astype(np.float32))

    def wt(w, nk):
        return np.ascontiguousarray(np.asarray(w, dtype=np.float32).T.reshape(nk, 128, -1))

    in_common = {
        "wqkv_t": wt(L0_attn_W_qkv, NKK),
        "wout_t": wt(L0_attn_W_out, NKK),
        "w1_t": wt(L0_ffn_w1_w, NKK),
        "w2_t": wt(L0_ffn_w2_w, NKK),
        "h1_t": wt(L0_ffn_h1_w, NOC),
        "h2_t": wt(L0_ffn_h2_w, NOC),
        "w3_t": wt(L0_ffn_w3_w, NOC),
    }
    s_all = np.zeros((128, 136), dtype=np.float32)
    s_all[:, 0:32] = np.asarray(L0_ffn_w1_s, dtype=np.float32).reshape(32, 128).T
    s_all[:, 32:64] = np.asarray(L0_ffn_w2_s, dtype=np.float32).reshape(32, 128).T
    s_all[:, 64:96] = np.asarray(L0_ffn_h1_s, dtype=np.float32).reshape(32, 128).T
    s_all[:, 96:128] = np.asarray(L0_ffn_h2_s, dtype=np.float32).reshape(32, 128).T
    s_all[:, 128:136] = np.asarray(L0_ffn_w3_s, dtype=np.float32).reshape(8, 128).T
    in_common["s_all"] = s_all

    nc = build_program(seq)
    in_maps = [dict(in_common, xT=xT[b]) for b in range(Bz)]
    res = run_bass_kernel_spmd(nc, in_maps, core_ids=list(range(Bz)))
    out = np.stack([np.transpose(r["outT"], (1, 0)) for r in res.results])
    return np.ascontiguousarray(out.astype(np.float32))


# version 12
# speedup vs baseline: 1.0238x; 1.0238x over previous
import numpy as np
import ml_dtypes
import concourse.bass as bass
import concourse.mybir as mybir
from concourse.bass_utils import run_bass_kernel_spmd

F32 = mybir.dt.float32
BF16 = mybir.dt.bfloat16
AF = mybir.ActivationFunctionType
ALU = mybir.AluOpType

E = 1024
NH = 16
HD = 64
NKK = 8
DHID = 4096
NOC = 32
GRP = 512
TILE = 1024


class Ctr:
    def __init__(self, name, owner, step):
        self.name = name
        self.owner = owner
        self.step = step
        self.sem = None
        self.v = 0
        self.is_dma = step == 16


class Buf:
    def __init__(self):
        self.wmark = None
        self.readers = []


SERIAL = False


class Plan:
    def __init__(self):
        self.steps = []
        self.ctrs = []

    def op(self, eng, fn, waits=(), inc=None):
        ws = []
        if SERIAL:
            for c in self.ctrs:
                if c.v > 0:
                    ws.append((c, c.v))
        for wv in waits:
            if wv is None:
                continue
            c, v = wv
            assert v <= c.v, f"wait on future mark {c.name} {v} > {c.v}"
            if v > 0:
                ws.append((c, v))
        self.steps.append((eng, ws, fn, inc))
        if inc is not None:
            inc.v += inc.step
            return (inc, inc.v)
        return None

    def read(self, eng, fn, bufs_r=(), waits=(), inc=None):
        ws = list(waits)
        for b in bufs_r:
            if b.wmark is not None:
                ws.append(b.wmark)
        m = self.op(eng, fn, ws, inc)
        if m is not None:
            for b in bufs_r:
                b.readers.append(m)
        return m

    def write(self, eng, fn, bufs_w=(), bufs_r=(), waits=(), inc=None):
        ws = list(waits)
        for b in bufs_w:
            ws.extend(b.readers)
            if b.wmark is not None:
                ws.append(b.wmark)
        for b in bufs_r:
            if b.wmark is not None:
                ws.append(b.wmark)
        m = self.op(eng, fn, ws, inc)
        if m is not None:
            for b in bufs_w:
                b.readers = []
                b.wmark = m
            for b in bufs_r:
                b.readers.append(m)
        else:
            for b in bufs_w:
                b.readers = []
                b.wmark = None
        return m


def _round32(x):
    return (x + 31) & ~31


def build_consts():
    cb = np.zeros((128, 288), dtype=ml_dtypes.bfloat16)
    cb[:, 0] = 1.0
    cb[:, 1:65] = 1.0
    eye = np.eye(64, dtype=ml_dtypes.bfloat16)
    cb[0:64, 65:129] = eye
    cb[64:128, 65:129] = eye
    cb[0, 129:257] = 1.0
    cf = np.zeros((128, 8), dtype=np.float32)
    cf[:, 0] = -1.0
    cf[:, 1] = 1e-5
    mk = np.zeros((128, 512), dtype=np.float32)
    for half in range(2):
        for p in range(128):
            mrow = half * 128 + p
            mk[p, half * 256 + mrow:half * 256 + 256] = 0.125
    return cb, cf, mk


def build_program(seq, debug=False):
    nG = seq // GRP
    nT = seq // TILE
    nc = bass.Bass("TRN2", target_bir_lowering=False)

    xT = nc.dram_tensor("xT", [E, seq], F32, kind="ExternalInput").ap()
    wqkv_t = nc.dram_tensor("wqkv_t", [NKK, 128, 3072], F32, kind="ExternalInput").ap()
    wout_t = nc.dram_tensor("wout_t", [NKK, 128, 1024], F32, kind="ExternalInput").ap()
    w1_t = nc.dram_tensor("w1_t", [NKK, 128, DHID], F32, kind="ExternalInput").ap()
    w2_t = nc.dram_tensor("w2_t", [NKK, 128, DHID], F32, kind="ExternalInput").ap()
    h1_t = nc.dram_tensor("h1_t", [NOC, 128, DHID], F32, kind="ExternalInput").ap()
    h2_t = nc.dram_tensor("h2_t", [NOC, 128, DHID], F32, kind="ExternalInput").ap()
    w3_t = nc.dram_tensor("w3_t", [NOC, 128, 1024], F32, kind="ExternalInput").ap()
    s_all_d = nc.dram_tensor("s_all", [128, 136], F32, kind="ExternalInput").ap()
    outT = nc.dram_tensor("outT", [E, seq], F32, kind="ExternalOutput").ap()
    x2scr = nc.dram_tensor("x2scr", [E, seq], F32,
                           kind="ExternalOutput" if debug else "Internal").ap()

    dbg = {}
    if debug:
        for nm, shp in (("dbg_h3", [128, 8192]), ("dbg_hA", [128, 32768]),
                        ("dbg_u1", [128, 32768]), ("dbg_ws", [128, 4096]),
                        ("dbg_z1", [128, 512]), ("dbg_s1o", [128, 512]),
                        ("dbg_hA0", [128, 512]),
                        ("dbg_g1", [128, 32768]), ("dbg_g2", [128, 32768]),
                        ("dbg_f", [128, 4096])):
            dbg[nm] = nc.dram_tensor(nm, shp, F32, kind="ExternalOutput").ap()

    cb_np, cf_np, mk_np = build_consts()
    cb_d = nc.inline_tensor(cb_np, "cb_const").ap()
    cf_d = nc.inline_tensor(cf_np, "cf_const").ap()
    mk_d = nc.inline_tensor(mk_np, "mk_const").ap()

    ARENA = 212096
    arena_base = nc.bump_sbuf(ARENA)[0]
    off = [0]

    def alloc(name, shape, dtype, at=None):
        nbytes = int(np.prod(shape[1:])) * mybir.dt.size(dtype)
        o = off[0] if at is None else at
        assert o + nbytes <= ARENA, (name, o, nbytes)
        t = nc.alloc_sbuf_tensor_at(name, shape, dtype, offset=arena_base + o)
        if at is None:
            off[0] = _round32(o + nbytes)
        return t

    consts_bf = alloc("consts_bf", [128, 288], BF16)
    consts_f = alloc("consts_f", [128, 8], F32)
    mask2 = alloc("mask2", [128, 512], F32)
    s_sb = alloc("s_sb", [128, 136], F32)
    stats_f = alloc("stats_f", [128, 1024], F32)
    stats_b = alloc("stats_b", [128, 2048], BF16)
    mrb_sb = alloc("mrb_sb", [128, 512], F32)
    rb_sb = alloc("rb_sb", [128, 512], F32)
    wslot = [alloc(f"wslot{i}", [128, 4096], BF16) for i in range(2)]

    base0 = off[0]
    xg = alloc("xg", [128, 8, 512], F32)
    bb1 = alloc("bb1", [128, 8, 512], BF16)
    bb2 = alloc("bb2", [128, 8, 512], BF16)
    h1s = alloc("h1s", [128, 8, 512], BF16)
    qTb = alloc("qTb", [128, 8, 512], BF16)
    kTb = alloc("kTb", [128, 8, 512], BF16)
    ktm = alloc("ktm", [128, 4, 1024], BF16)
    vtm = alloc("vtm", [128, 4, 1024], BF16)
    numh = alloc("numh", [128, 8, 512], BF16)
    a_sb = alloc("a_sb", [128, 8, 512], F32)
    attp_t = [alloc(f"attp{i}", [128, 512], BF16) for i in range(2)]
    attp = [attp_t[0], attp_t[1], attp_t[0], attp_t[1]]
    tmp = [alloc(f"tmp{i}", [128, 512], F32) for i in range(4)]
    S_bf = alloc("S_bf", [128, 1024], BF16)
    S_hi = alloc("S_hi", [128, 512], BF16)
    Z_hi = alloc("Z_hi", [128, 16], BF16)
    Z_rep = alloc("Z_rep", [128, 1024], BF16)
    wqkv_sb = alloc("wqkv_sb", [128, 8, 3072], BF16)
    xg2 = alloc("xg2", [128, 8, 512], F32)
    p1_end = off[0]

    off[0] = base0
    hidA = alloc("hidA", [128, 32, 1024], BF16)
    hidB = alloc("hidB", [128, 32, 1024], BF16)
    flex0 = off[0]
    h3s = alloc("h3s", [128, 8, 1024], BF16)
    fsb = alloc("fsb", [128, 8, 512], F32, at=flex0)
    off[0] = _round32(flex0 + 16384)
    x2g_off = off[0]
    x2g = alloc("x2g", [128, 8, 512], F32)
    wslot2 = alloc("wslot2", [128, 4096], BF16, at=x2g_off)
    bbflex0 = off[0]
    bb1b = alloc("bb1b", [128, 8, 512], BF16)
    bb2b = alloc("bb2b", [128, 8, 512], BF16)
    x2g2 = alloc("x2g2", [128, 8, 512], F32, at=bbflex0)
    p2_end = off[0]
    assert max(p1_end, p2_end) <= ARENA, (p1_end, p2_end)

    pb = [nc.alloc_psum_tensor(f"pb{i}", [128, 512], F32) for i in range(8)]

    c_pe = Ctr("c_pe", "pe", 1)
    c_ve = Ctr("c_ve", "ve", 1)
    c_ac = Ctr("c_ac", "ac", 1)
    c_gp = Ctr("c_gp", "gp", 16)
    c_sy = Ctr("c_sy", "sy", 16)

    P = Plan()
    P.ctrs = [c_pe, c_ve, c_ac, c_gp, c_sy]

    bXG, bBB1, bBB2, bH1, bQT, bKT, bKTM, bVTM = (Buf() for _ in range(8))
    bNUM, bASB, bSBF, bSHI, bZHI, bZREP, bWQKV, bXG2 = (Buf() for _ in range(8))
    bSTF, bSTB, bMRB, bRB = (Buf() for _ in range(4))
    bHIDA, bHIDB = Buf(), Buf()
    bH3 = Buf()
    bH3g = [Buf(), Buf()]
    bX2G = Buf()
    bFLXB = Buf()
    bATT_t = [Buf(), Buf()]
    bATT = [bATT_t[0], bATT_t[1], bATT_t[0], bATT_t[1]]
    bTMP = [Buf() for _ in range(4)]
    bPB = [Buf() for _ in range(8)]
    bWS = [Buf() for _ in range(2)]

    ones_col = consts_bf[:, 0:1]
    ones_blk = consts_bf[:, 1:65]
    id_lo = consts_bf[0:64, 65:129]
    id_hi = consts_bf[64:128, 65:129]
    ones_row = consts_bf[0:1, 129:257]
    negone = consts_f[:, 0:1]
    eps_col = consts_f[0:1, 1:2]

    m_c0 = P.op("gp", lambda e: e.dma_start(consts_bf[:, :], cb_d), inc=c_gp)
    P.op("gp", lambda e: e.dma_start(consts_f[:, :], cf_d), inc=c_gp)
    P.op("gp", lambda e: e.dma_start(mask2[:, :], mk_d), inc=c_gp)
    m_consts = P.op("gp", lambda e: e.dma_start(s_sb[:, :], s_all_d), inc=c_gp)
    del m_c0
    P.write("gp", lambda e: e.dma_start(
        wqkv_sb[:, :, :], wqkv_t.rearrange("k p o -> p k o")), bufs_w=[bWQKV], inc=c_gp)

    def accum_mms(make_fn, nacc, bank_buf, src_bufs, extra_waits=()):
        for i in range(nacc):
            fn = make_fn(i)
            if i == 0:
                P.write("pe", fn, bufs_w=[bank_buf], bufs_r=list(src_bufs),
                        waits=list(extra_waits),
                        inc=(c_pe if nacc == 1 else None))
            elif i == nacc - 1:
                P.read("pe", fn, bufs_r=[bank_buf] + list(src_bufs), inc=c_pe)
            else:
                P.read("pe", fn, bufs_r=[bank_buf])
        bank_buf.wmark = (c_pe, c_pe.v)
        bank_buf.readers = []

    def emit_elu(bank_buf, bank_t, dst_buf, dst_view, ti):
        tA, tB = bTMP[2 * ti], bTMP[2 * ti + 1]
        tAa, tBa = tmp[2 * ti], tmp[2 * ti + 1]
        P.write("ac", lambda e: e.activation(tAa[:, :], bank_t[:, 0:512],
                                             AF.Relu, scale=negone),
                bufs_w=[tA], bufs_r=[bank_buf], waits=[m_consts], inc=c_ac)
        P.write("ac", lambda e: e.activation(tAa[:, :], tAa[:, :],
                                             AF.Exp, scale=negone),
                bufs_w=[tA], inc=c_ac)
        P.write("ac", lambda e: e.activation(tBa[:, :], bank_t[:, 0:512], AF.Relu),
                bufs_w=[tB], bufs_r=[bank_buf], inc=c_ac)
        P.write("ve", lambda e: e.tensor_add(dst_view, tAa[:, :], tBa[:, :]),
                bufs_w=[dst_buf], bufs_r=[tA, tB], inc=c_ve)

    def emit_ln(src_buf, src_t, mode, out_buf=None, out_slice=None, dst_buf=None,
                dst_t=None, bb=None, out_bufs=None, store_t0=None):
        if out_bufs is None:
            out_bufs = [out_buf] if out_buf is not None else []
        (b1buf, b1t, b2buf, b2t) = bb
        P.write("ac", lambda e: e.activation(b1t[:, :, :], src_t[:, :, :], AF.Copy),
                bufs_w=[b1buf], bufs_r=[src_buf], inc=c_ac)
        P.write("ve", lambda e: e.tensor_mul(b2t[:, :, :], src_t[:, :, :],
                                             src_t[:, :, :]),
                bufs_w=[b2buf], bufs_r=[src_buf], inc=c_ve)
        for bank_buf, bank_t, bsbuf, bst in ((bPB[0], pb[0], b1buf, b1t),
                                             (bPB[1], pb[1], b2buf, b2t)):
            accum_mms(lambda kk, bt=bank_t, bs=bst: (
                lambda e: e.matmul(bt[0:1, 0:512], ones_col, bs[:, kk, :],
                                   start=(kk == 0), stop=(kk == NKK - 1))),
                NKK, bank_buf, [bsbuf], extra_waits=[m_consts])
        mcol = stats_f[0:1, 0:512]
        vcol = stats_f[0:1, 512:1024]
        rcol = mrb_sb[0:1, 0:512]
        P.write("ve", lambda e: e.tensor_scalar_mul(mcol, pb[0][0:1, 0:512], 1.0 / E),
                bufs_w=[bSTF], bufs_r=[bPB[0]], inc=c_ve)
        P.write("ve", lambda e: e.tensor_mul(vcol, mcol, mcol),
                bufs_w=[bSTF], inc=c_ve)
        P.write("ve", lambda e: e.scalar_tensor_tensor(
            vcol, pb[1][0:1, 0:512], 1.0 / E, vcol, ALU.mult, ALU.subtract),
                bufs_w=[bSTF], bufs_r=[bPB[1]], inc=c_ve)
        P.write("ac", lambda e: e.activation(vcol, vcol, AF.Sqrt, bias=eps_col),
                bufs_w=[bSTF], waits=[m_consts], inc=c_ac)
        P.write("ve", lambda e: e.reciprocal(rcol, vcol),
                bufs_w=[bMRB], bufs_r=[bSTF], inc=c_ve)
        P.write("ve", lambda e: e.tensor_mul(mcol, mcol, rcol),
                bufs_w=[bSTF], bufs_r=[bMRB], inc=c_ve)
        P.write("ve", lambda e: e.tensor_copy(stats_b[0:1, 0:512], rcol),
                bufs_w=[bSTB], bufs_r=[bMRB], inc=c_ve)
        P.write("ve", lambda e: e.tensor_copy(vcol, stats_b[0:1, 0:512]),
                bufs_w=[bSTF], inc=c_ve)
        P.write("ve", lambda e: e.tensor_sub(stats_b[0:1, 512:1024], rcol, vcol),
                bufs_w=[bSTB], bufs_r=[bMRB, bSTF], inc=c_ve)
        P.write("ve", lambda e: e.tensor_copy(stats_b[0:1, 1024:1536], mcol),
                bufs_w=[bSTB], bufs_r=[bSTF], inc=c_ve)
        P.write("ve", lambda e: e.tensor_copy(vcol, stats_b[0:1, 1024:1536]),
                bufs_w=[bSTF], inc=c_ve)
        P.write("ve", lambda e: e.tensor_sub(stats_b[0:1, 1536:2048], mcol, vcol),
                bufs_w=[bSTB], bufs_r=[bSTF], inc=c_ve)
        accum_mms(lambda i: (
            lambda e, i=i: e.matmul(pb[0][:, 0:512], ones_row,
                                    stats_b[0:1, 1024 + i * 512:1536 + i * 512],
                                    start=(i == 0), stop=(i == 1))),
            2, bPB[0], [bSTB])
        accum_mms(lambda i: (
            lambda e, i=i: e.matmul(pb[1][:, 0:512], ones_row,
                                    stats_b[0:1, i * 512:512 + i * 512],
                                    start=(i == 0), stop=(i == 1))),
            2, bPB[1], [bSTB])
        P.write("ac", lambda e: e.activation(mrb_sb[:, :], pb[0][:, 0:512], AF.Copy),
                bufs_w=[bMRB], bufs_r=[bPB[0]], inc=c_ac)
        P.write("ve", lambda e: e.tensor_copy(rb_sb[:, :], pb[1][:, 0:512]),
                bufs_w=[bRB], bufs_r=[bPB[1]], inc=c_ve)
        for cch in range(8):
            P.write("ve", lambda e, c=cch: e.tensor_mul(
                tmp[0][:, :], src_t[:, c, :], rb_sb[:, :]),
                    bufs_w=[bTMP[0]], bufs_r=[src_buf, bRB], inc=c_ve)
            if mode == "h":
                P.write("ve", lambda e, c=cch: e.tensor_sub(
                    out_slice(c), tmp[0][:, :], mrb_sb[:, :]),
                        bufs_w=out_bufs, bufs_r=[bTMP[0], bMRB], inc=c_ve)
            else:
                P.write("ve", lambda e, c=cch: e.tensor_sub(
                    tmp[1][:, :], tmp[0][:, :], mrb_sb[:, :]),
                        bufs_w=[bTMP[1]], bufs_r=[bTMP[0], bMRB], inc=c_ve)
                P.write("ve", lambda e, c=cch: e.tensor_add(
                    dst_t[:, c, :], dst_t[:, c, :], tmp[1][:, :]),
                        bufs_w=[dst_buf], bufs_r=[bTMP[1]], inc=c_ve)
                if store_t0 is not None:
                    P.read("sy", lambda e, c=cch, t0=store_t0: e.dma_start(
                        x2scr[c * 128:(c + 1) * 128, t0:t0 + GRP],
                        dst_t[:, c, :]), bufs_r=[dst_buf], inc=c_sy)

    wslot3 = wslot + [wslot2]
    bWS3 = bWS + [bX2G]

    def load_wchunk(slot_i, src_ap):
        nk = src_ap.shape[0]
        return P.write("gp", lambda e, s=wslot3[slot_i], a=src_ap, nk=nk: e.dma_start(
            s[:, :].rearrange("p (k o) -> p k o", k=nk), a.rearrange("k p o -> p k o")),
                       bufs_w=[bWS3[slot_i]], inc=c_gp)

    x2st_marks = []
    s_started = [False, False]
    z_started = [False]
    xgs = [xg, xg2]
    bXGs = [bXG, bXG2]

    def load_x(g):
        t0 = g * GRP
        P.write("sy", lambda e, t0=t0, xb=xgs[g % 2]: e.dma_start(
            xb[:, :, :], xT[:, t0:t0 + GRP].rearrange("(c p) t -> p c t", p=128)),
                bufs_w=[bXGs[g % 2]],
                waits=([(c_sy, 16)] if g == 1 else []), inc=c_sy)

    load_x(0)
    for g in range(nG):
        t0 = g * GRP
        xgc = xgs[g % 2]
        bXGc = bXGs[g % 2]
        emit_ln(bXGc, xgc, "h", out_buf=bH1,
                out_slice=lambda c: h1s[:, c, :], bb=(bBB1, bb1, bBB2, bb2))
        for oc in range(16):
            bi = 2 + (oc % 2)
            accum_mms(lambda kk, bi=bi, oc=oc: (
                lambda e: e.matmul(pb[bi][:, 0:512],
                                   wqkv_sb[:, kk, oc * 128:(oc + 1) * 128],
                                   h1s[:, kk, :],
                                   start=(kk == 0), stop=(kk == NKK - 1))),
                NKK, bPB[bi], [bH1, bWQKV])
            if oc < 8:
                emit_elu(bPB[bi], pb[bi], bQT, qTb[:, oc, :], oc % 2)
            else:
                emit_elu(bPB[bi], pb[bi], bKT, kTb[:, oc - 8, :], oc % 2)
        for mc in range(4):
            for which in ("v", "k"):
                for half in range(2):
                    bi = 4 + half
                    ocol = (2048 if which == "v" else 1024) + half * 512
                    accum_mms(lambda kk, bi=bi, mc=mc, ocol=ocol: (
                        lambda e: e.matmul(pb[bi][:, 0:512],
                                           h1s[:, kk, mc * 128:(mc + 1) * 128],
                                           wqkv_sb[:, kk, ocol:ocol + 512],
                                           start=(kk == 0), stop=(kk == NKK - 1))),
                        NKK, bPB[bi], [bH1, bWQKV])
                    if which == "v":
                        P.write("ac", lambda e, bi=bi, mc=mc, half=half: e.activation(
                            vtm[:, mc, half * 512:(half + 1) * 512],
                            pb[bi][:, 0:512], AF.Copy),
                                bufs_w=[bVTM], bufs_r=[bPB[bi]], inc=c_ac)
                    else:
                        emit_elu(bPB[bi], pb[bi], bKTM,
                                 ktm[:, mc, half * 512:(half + 1) * 512], half)
        if g + 1 < nG:
            load_x(g + 1)
        mwo = [load_wchunk(i, wout_t[:, :, i * 512:(i + 1) * 512]) for i in range(2)]
        for cc in range(2):
            C = 2 * g + cc
            if C > 0:
                P.write("ve", lambda e: e.tensor_scalar_mul(
                    S_bf[0:64, 0:512], pb[6][0:64, 0:512], 0.125),
                        bufs_w=[bSBF], bufs_r=[bPB[6]], inc=c_ve)
                P.write("ve", lambda e: e.tensor_scalar_mul(
                    S_bf[0:64, 512:1024], pb[7][0:64, 0:512], 0.125),
                        bufs_w=[bSBF], bufs_r=[bPB[7]], inc=c_ve)
                P.write("ve", lambda e: e.tensor_scalar_mul(
                    Z_hi[64:128, 0:16], pb[6][64:128, 0:16], 0.125),
                        bufs_w=[bZHI], bufs_r=[bPB[6]], inc=c_ve)
                P.write("ve", lambda e: e.tensor_copy(
                    Z_rep[64:128, :].rearrange("p (h r) -> p h r", h=16),
                    bass.AP(Z_hi, 64 * 16, [[16, 64], [1, 16], [0, 64]])),
                        bufs_w=[bZREP], bufs_r=[bZHI], inc=c_ve)
                srhs = bass.AP(S_bf, 64, [[1024, 64], [128, 8], [1, 64]])
                accum_mms(lambda i, srhs=srhs: (
                    lambda e: e.matmul(pb[1][64:128, 0:512], id_lo, srhs,
                                       start=True, stop=True)),
                    1, bPB[1], [bSBF], extra_waits=[m_consts])
                P.write("ve", lambda e: e.tensor_copy(
                    S_hi[64:128, 0:512], pb[1][64:128, 0:512]),
                        bufs_w=[bSHI], bufs_r=[bPB[1]], inc=c_ve)
                accum_mms(lambda i: (
                    lambda e: e.matmul(pb[0][0:64, 0:16], id_hi,
                                       Z_hi[64:128, 0:16], start=True, stop=True)),
                    1, bPB[0], [bZHI], extra_waits=[m_consts])
                P.write("ve", lambda e: e.tensor_copy(
                    Z_rep[0:64, :].rearrange("p (h r) -> p h r", h=16),
                    bass.AP(pb[0], 0, [[512, 64], [1, 16], [0, 64]])),
                        bufs_w=[bZREP], bufs_r=[bPB[0]], inc=c_ve)
            for q4 in range(4):
                nbi = 2 if q4 % 2 == 0 else 0
                dbi = 3 if q4 % 2 == 0 else 1
                for hh in range(4):
                    h = q4 * 4 + hh
                    hc, hp0, par, qcol = h // 2, (h % 2) * 64, h % 2, hh // 2
                    sbi = 4 + (h % 2)
                    accum_mms(lambda half, sbi=sbi, h=h, hc=hc, hp0=hp0, cc=cc: (
                        lambda e: e.matmul(
                            pb[sbi][:, half * 256:(half + 1) * 256],
                            kTb[hp0:hp0 + 64, hc,
                                cc * 256 + half * 128:cc * 256 + half * 128 + 128],
                            qTb[hp0:hp0 + 64, hc, cc * 256:(cc + 1) * 256],
                            start=(half == 0), stop=(half == 1))),
                        2, bPB[sbi], [bKT, bQT])
                    P.write("ve", lambda e, hh=hh, sbi=sbi: e.tensor_mul(
                        attp[hh][:, :], pb[sbi][:, 0:512], mask2[:, :]),
                            bufs_w=[bATT[hh]], bufs_r=[bPB[sbi]],
                            waits=[m_consts], inc=c_ve)
                    nsrc = [bVTM, bATT[hh]]
                    nacc = 3 if C > 0 else 2

                    def mk_num(i, h=h, hh=hh, cc=cc, par=par, qcol=qcol, C=C,
                               hp0=hp0, hc=hc, nbi=nbi):
                        if i < 2:
                            return (lambda e, half=i: e.matmul(
                                pb[nbi][par * 64:par * 64 + 64,
                                        qcol * 256:qcol * 256 + 256],
                                vtm[:, 2 * cc + half, h * 64:h * 64 + 64],
                                attp[hh][:, half * 256:(half + 1) * 256],
                                start=(hh < 2 and i == 0), stop=False))
                        if par == 0:
                            slhs = S_bf[0:64, h * 64:h * 64 + 64]
                        else:
                            slhs = S_hi[64:128, (h // 2) * 64:(h // 2) * 64 + 64]
                        return (lambda e, slhs=slhs: e.matmul(
                            pb[nbi][par * 64:par * 64 + 64,
                                    qcol * 256:qcol * 256 + 256],
                            slhs, qTb[hp0:hp0 + 64, hc, cc * 256:(cc + 1) * 256],
                            start=False, stop=True))
                    if C > 0:
                        nsrc = nsrc + ([bSBF] if par == 0 else [bSHI]) + [bQT]
                    for i in range(nacc):
                        fn = mk_num(i)
                        if hh < 2 and i == 0:
                            P.write("pe", fn, bufs_w=[bPB[nbi]], bufs_r=nsrc)
                        elif i == 0:
                            P.read("pe", fn, bufs_r=[bPB[nbi]] + nsrc)
                        elif i == nacc - 1:
                            P.read("pe", fn, bufs_r=[bPB[nbi]] + nsrc, inc=c_pe)
                        else:
                            P.read("pe", fn, bufs_r=[bPB[nbi]])
                    bPB[nbi].wmark = (c_pe, c_pe.v)
                    bPB[nbi].readers = []
                    dsrc = [bATT[hh]]
                    if C > 0:
                        dsrc = dsrc + [bZREP, bQT]

                    def mk_den(i, h=h, hh=hh, cc=cc, par=par, qcol=qcol, C=C,
                               hp0=hp0, hc=hc, dbi=dbi):
                        if i < 2:
                            return (lambda e, half=i: e.matmul(
                                pb[dbi][par * 64:par * 64 + 64,
                                        qcol * 256:qcol * 256 + 256],
                                ones_blk, attp[hh][:, half * 256:(half + 1) * 256],
                                start=(hh < 2 and i == 0), stop=False))
                        zlhs = Z_rep[par * 64:par * 64 + 64, h * 64:h * 64 + 64]
                        return (lambda e, zlhs=zlhs: e.matmul(
                            pb[dbi][par * 64:par * 64 + 64,
                                    qcol * 256:qcol * 256 + 256],
                            zlhs, qTb[hp0:hp0 + 64, hc, cc * 256:(cc + 1) * 256],
                            start=False, stop=True))
                    for i in range(nacc):
                        fn = mk_den(i)
                        if hh < 2 and i == 0:
                            P.write("pe", fn, bufs_w=[bPB[dbi]], bufs_r=dsrc,
                                    waits=[m_consts])
                        elif i == 0:
                            P.read("pe", fn, bufs_r=[bPB[dbi]] + dsrc)
                        elif i == nacc - 1:
                            P.read("pe", fn, bufs_r=[bPB[dbi]] + dsrc, inc=c_pe)
                        else:
                            P.read("pe", fn, bufs_r=[bPB[dbi]])
                    bPB[dbi].wmark = (c_pe, c_pe.v)
                    bPB[dbi].readers = []
                ti = q4 % 2
                P.write("ve", lambda e, dbi=dbi, ti=ti: e.tensor_scalar_max(
                    tmp[2 * ti][:, :], pb[dbi][:, 0:512], 1e-6),
                        bufs_w=[bTMP[2 * ti]], bufs_r=[bPB[dbi]], inc=c_ve)
                P.write("ve", lambda e, ti=ti: e.reciprocal(
                    tmp[2 * ti + 1][:, :], tmp[2 * ti][:, :]),
                        bufs_w=[bTMP[2 * ti + 1]], bufs_r=[bTMP[2 * ti]], inc=c_ve)
                P.write("ve", lambda e, q4=q4, cc=cc, nbi=nbi, ti=ti: e.tensor_mul(
                    bass.AP(numh, (2 * q4) * 512 + cc * 256,
                            [[4096, 128], [512, 2], [1, 256]]),
                    bass.AP(pb[nbi], 0, [[512, 128], [256, 2], [1, 256]]),
                    bass.AP(tmp[2 * ti + 1], 0, [[512, 128], [256, 2], [1, 256]])),
                        bufs_w=[bNUM], bufs_r=[bPB[nbi], bTMP[2 * ti + 1]], inc=c_ve)
            upd_srcs = [bKTM, bVTM]
            n_upd = NH * 4
            idx = 0
            for h in range(NH):
                si = 6 if h < 8 else 7
                sbuf = bPB[si]
                for half in range(2):
                    st = not s_started[si - 6]
                    s_started[si - 6] = True
                    last = False
                    fn = (lambda e, si=si, h=h, half=half, cc=cc, st=st:
                          e.matmul(pb[si][0:64, (h % 8) * 64:(h % 8) * 64 + 64],
                                   ktm[:, 2 * cc + half, h * 64:h * 64 + 64],
                                   vtm[:, 2 * cc + half, h * 64:h * 64 + 64],
                                   start=st, stop=False))
                    P.write("pe", fn, bufs_w=[sbuf], bufs_r=[bKTM, bVTM])
                for half in range(2):
                    st = not z_started[0]
                    z_started[0] = True
                    last = h == NH - 1 and half == 1
                    fn = (lambda e, h=h, half=half, cc=cc, st=st:
                          e.matmul(pb[6][64:128, h:h + 1],
                                   ktm[:, 2 * cc + half, h * 64:h * 64 + 64],
                                   ones_col, start=st, stop=False))
                    if last:
                        P.read("pe", fn, bufs_r=[bKTM, bVTM], waits=[m_consts],
                               inc=c_pe)
                    else:
                        P.write("pe", fn, bufs_w=[bPB[6]], bufs_r=[bKTM],
                                waits=[m_consts])
            mupd = (c_pe, c_pe.v)
            for si in (6, 7):
                bPB[si].wmark = mupd
                bPB[si].readers = []
        for oc in range(8):
            bi = 2 + (oc % 4)
            sl = oc // 4
            accum_mms(lambda kk, bi=bi, oc=oc, sl=sl: (
                lambda e: e.matmul(pb[bi][:, 0:512],
                                   bass.AP(wslot3[sl], kk * 512 + (oc % 4) * 128,
                                           [[4096, 128], [1, 128]]),
                                   numh[:, kk, :],
                                   start=(kk == 0), stop=(kk == NKK - 1))),
                NKK, bPB[bi], [bNUM, bWS3[sl]], extra_waits=[mwo[sl]])
            P.write("ac", lambda e, bi=bi, oc=oc: e.activation(
                a_sb[:, oc, :], pb[bi][:, 0:512], AF.Copy),
                    bufs_w=[bASB], bufs_r=[bPB[bi]], inc=c_ac)
        emit_ln(bASB, a_sb, "res", dst_buf=bXGc, dst_t=xgc,
                bb=(bBB1, bb1, bBB2, bb2))
        m = P.read("sy", lambda e, t0=t0, xb=xgc: e.dma_start(
            x2scr[:, t0:t0 + GRP].rearrange("(c p) t -> p c t", p=128), xb[:, :, :]),
                   bufs_r=[bXGc], inc=c_sy)
        x2st_marks.append(m)

    wslot3 = wslot + [wslot2]
    bWS3 = bWS + [bX2G]

    def load_wchunk(slot_i, src_ap):
        nk = src_ap.shape[0]
        return P.write("gp", lambda e, s=wslot3[slot_i], a=src_ap, nk=nk: e.dma_start(
            s[:, :].rearrange("p (k o) -> p k o", k=nk), a.rearrange("k p o -> p k o")),
                       bufs_w=[bWS3[slot_i]], inc=c_gp)

    m_phase1 = x2st_marks[-1]
    wch_i = [0]

    def next_slot():
        slot = wch_i[0] % 3
        wch_i[0] += 1
        return slot

    for T in range(nT):
        tt0 = T * TILE
        for g2 in range(2):
            t0 = tt0 + g2 * GRP
            ld_waits = [x2st_marks[t0 // GRP]]
            if T == 0 and g2 == 0:
                ld_waits.append(m_phase1)
            P.write("sy", lambda e, t0=t0: e.dma_start(
                x2g[:, :, :], x2scr[:, t0:t0 + GRP].rearrange("(c p) t -> p c t", p=128)),
                    bufs_w=[bX2G], waits=ld_waits, inc=c_sy)
            emit_ln(bX2G, x2g, "h", out_bufs=[bH3g[g2], bH3],
                    out_slice=lambda c, g2=g2: bass.AP(
                        h3s, g2 * 512 + c * 1024, [[8192, 128], [1, 512]]),
                    bb=(bFLXB, bb1b, bFLXB, bb2b))
        if debug and T == 0:
            P.read("gp", lambda e: e.dma_start(
                dbg["dbg_h3"].rearrange("p (k t) -> p k t", k=8), h3s[:, :, :]),
                   bufs_r=[bH3], inc=c_gp)
        for stage, wt, scol in (("c1", w1_t, 0), ("c2", w2_t, 32)):
            for j in range(8):
                slot = next_slot()
                mw = load_wchunk(slot, wt[:, :, j * 512:(j + 1) * 512])
                for oc2 in range(4):
                    og = j * 4 + oc2
                    for g2 in range(2):
                        bi = 2 + ((2 * oc2 + g2) % 4)
                        accum_mms(lambda kk, bi=bi, slot=slot, oc2=oc2, g2=g2: (
                            lambda e: e.matmul(
                                pb[bi][:, 0:512],
                                bass.AP(wslot3[slot], kk * 512 + oc2 * 128,
                                        [[4096, 128], [1, 128]]),
                                h3s[:, kk, g2 * 512:(g2 + 1) * 512],
                                start=(kk == 0), stop=(kk == NKK - 1))),
                            NKK, bPB[bi], [bH3g[g2], bWS3[slot]], extra_waits=[mw])
                        if stage == "c1":
                            if debug and T == 0 and j == 0 and oc2 == 0 and g2 == 0:
                                P.write("ac", lambda e, bi=bi: e.activation(
                                    tmp[2][:, :], pb[bi][:, 0:512], AF.Copy),
                                        bufs_w=[bTMP[2]], bufs_r=[bPB[bi]], inc=c_ac)
                                P.read("gp", lambda e: e.dma_start(
                                    dbg["dbg_z1"], tmp[2][:, :]),
                                       bufs_r=[bTMP[2]], inc=c_gp)
                                P.read("gp", lambda e: e.dma_start(
                                    dbg["dbg_ws"], wslot[0][:, :]),
                                       bufs_r=[bWS[0]], inc=c_gp)
                                P.write("ac", lambda e, bi=bi: e.activation(
                                    tmp[3][:, :], pb[bi][:, 0:512], AF.Silu,
                                    scale=s_sb[:, 0:1]),
                                        bufs_w=[bTMP[3]], bufs_r=[bPB[bi]], inc=c_ac)
                                P.read("gp", lambda e: e.dma_start(
                                    dbg["dbg_s1o"], tmp[3][:, :]),
                                       bufs_r=[bTMP[3]], inc=c_gp)
                            P.write("ac", lambda e, bi=bi, og=og, g2=g2, scol=scol:
                                    e.activation(
                                hidA[:, og, g2 * 512:(g2 + 1) * 512], pb[bi][:, 0:512],
                                AF.Silu, scale=s_sb[:, scol + og:scol + og + 1]),
                                    bufs_w=[bHIDA], bufs_r=[bPB[bi]],
                                    waits=[m_consts], inc=c_ac)
                            if debug and T == 0 and j == 0 and oc2 == 0 and g2 == 0:
                                P.read("gp", lambda e: e.dma_start(
                                    dbg["dbg_hA0"], hidA[:, 0, 0:512]),
                                       bufs_r=[bHIDA], inc=c_gp)
                        else:
                            P.write("ac", lambda e, bi=bi, og=og, g2=g2, scol=scol:
                                    e.activation(
                                hidB[:, og, g2 * 512:(g2 + 1) * 512], pb[bi][:, 0:512],
                                AF.Copy, scale=s_sb[:, scol + og:scol + og + 1]),
                                    bufs_w=[bHIDB], bufs_r=[bPB[bi]],
                                    waits=[m_consts], inc=c_ac)
                            P.write("ve", lambda e, og=og, g2=g2: e.tensor_mul(
                                hidA[:, og, g2 * 512:(g2 + 1) * 512],
                                hidA[:, og, g2 * 512:(g2 + 1) * 512],
                                hidB[:, og, g2 * 512:(g2 + 1) * 512]),
                                    bufs_w=[bHIDA], bufs_r=[bHIDB], inc=c_ve)
            if debug and T == 0 and stage == "c1":
                P.read("gp", lambda e: e.dma_start(
                    dbg["dbg_u1"].rearrange("p (k t) -> p k t", k=32), hidA[:, :, :]),
                       bufs_r=[bHIDA], inc=c_gp)
        if debug and T == 0:
            P.read("gp", lambda e: e.dma_start(
                dbg["dbg_hA"].rearrange("p (k t) -> p k t", k=32), hidA[:, :, :]),
                   bufs_r=[bHIDA], inc=c_gp)
        if False:
            pass
        for stage, wt, scol, src, srcb, dst, dstb in (
                ("d", h1_t, 64, hidA, bHIDA, hidB, bHIDB),
                ("e", h2_t, 96, hidB, bHIDB, hidA, bHIDA)):
            for ob in range(8):
                for kq in range(4):
                    slot = next_slot()
                    mw = load_wchunk(slot, wt[kq * 8:(kq + 1) * 8, :,
                                               ob * 512:(ob + 1) * 512])
                    for oc2 in range(4):
                        for g2 in range(2):
                            bi = oc2 * 2 + g2
                            for kk in range(8):
                                fn = (lambda e, bi=bi, slot=slot, oc2=oc2, g2=g2,
                                      kk=kk, kq=kq, src=src: e.matmul(
                                          pb[bi][:, 0:512],
                                          bass.AP(wslot3[slot],
                                                  kk * 512 + oc2 * 128,
                                                  [[4096, 128], [1, 128]]),
                                          src[:, kq * 8 + kk,
                                              g2 * 512:(g2 + 1) * 512],
                                          start=(kq == 0 and kk == 0),
                                          stop=(kq == 3 and kk == 7)))
                                if kq == 0 and kk == 0:
                                    P.write("pe", fn, bufs_w=[bPB[bi]],
                                            bufs_r=[srcb, bWS3[slot]], waits=[mw])
                                elif kk == 7:
                                    P.read("pe", fn,
                                           bufs_r=[bPB[bi], srcb, bWS3[slot]],
                                           inc=c_pe)
                                    bPB[bi].wmark = (c_pe, c_pe.v)
                                    bPB[bi].readers = []
                                else:
                                    P.read("pe", fn, bufs_r=[bPB[bi]],
                                           waits=[mw] if kk == 0 else [])
                for oc2 in range(4):
                    for g2 in range(2):
                        bi = oc2 * 2 + g2
                        j = ob * 4 + oc2
                        P.write("ac", lambda e, bi=bi, j=j, g2=g2, dst=dst,
                                scol=scol: e.activation(
                            dst[:, j, g2 * 512:(g2 + 1) * 512], pb[bi][:, 0:512],
                            AF.Gelu, scale=s_sb[:, scol + j:scol + j + 1]),
                                bufs_w=[dstb], bufs_r=[bPB[bi]],
                                waits=[m_consts], inc=c_ac)
        if debug and T == 0:
            P.read("gp", lambda e: e.dma_start(
                dbg["dbg_g1"].rearrange("p (k t) -> p k t", k=32), hidB[:, :, :]),
                   bufs_r=[bHIDB], inc=c_gp)
            P.read("gp", lambda e: e.dma_start(
                dbg["dbg_g2"].rearrange("p (k t) -> p k t", k=32), hidA[:, :, :]),
                   bufs_r=[bHIDA], inc=c_gp)
        for g2 in range(2):
            t0 = tt0 + g2 * GRP
            for ob in range(2):
                for kq in range(4):
                    slot = next_slot()
                    mw = load_wchunk(slot, w3_t[kq * 8:(kq + 1) * 8, :,
                                               ob * 512:(ob + 1) * 512])
                    for oc2 in range(4):
                        bi = 2 + oc2
                        for kk in range(8):
                            fn = (lambda e, bi=bi, slot=slot, oc2=oc2, g2=g2,
                                  kk=kk, kq=kq: e.matmul(
                                      pb[bi][:, 0:512],
                                      bass.AP(wslot3[slot], kk * 512 + oc2 * 128,
                                              [[4096, 128], [1, 128]]),
                                      hidA[:, kq * 8 + kk,
                                           g2 * 512:(g2 + 1) * 512],
                                      start=(kq == 0 and kk == 0),
                                      stop=(kq == 3 and kk == 7)))
                            if kq == 0 and kk == 0:
                                P.write("pe", fn, bufs_w=[bPB[bi]],
                                        bufs_r=[bHIDA, bWS3[slot]], waits=[mw])
                            elif kk == 7:
                                P.read("pe", fn,
                                       bufs_r=[bPB[bi], bHIDA, bWS3[slot]],
                                       inc=c_pe)
                                bPB[bi].wmark = (c_pe, c_pe.v)
                                bPB[bi].readers = []
                            else:
                                P.read("pe", fn, bufs_r=[bPB[bi]],
                                       waits=[mw] if kk == 0 else [])
                for oc2 in range(4):
                    bi = 2 + oc2
                    oc = ob * 4 + oc2
                    P.write("ac", lambda e, bi=bi, oc=oc: e.activation(
                        fsb[:, oc, :], pb[bi][:, 0:512], AF.Copy,
                        scale=s_sb[:, 128 + oc:128 + oc + 1]),
                            bufs_w=[bH3], bufs_r=[bPB[bi]],
                            waits=[m_consts], inc=c_ac)
            if debug and T == 0 and g2 == 0:
                P.read("gp", lambda e: e.dma_start(
                    dbg["dbg_f"].rearrange("p (k t) -> p k t", k=8), fsb[:, :, :]),
                       bufs_r=[bH3], inc=c_gp)
            P.write("ac", lambda e: e.activation(bb1b[:, :, :], fsb[:, :, :], AF.Copy),
                    bufs_w=[bFLXB], bufs_r=[bH3], inc=c_ac)
            P.write("ve", lambda e: e.tensor_mul(bb2b[:, :, :], fsb[:, :, :],
                                                 fsb[:, :, :]),
                    bufs_w=[bFLXB], bufs_r=[bH3], inc=c_ve)
            for bank_buf, bank_t, bst in ((bPB[0], pb[0], bb1b), (bPB[1], pb[1], bb2b)):
                accum_mms(lambda kk, bt=bank_t, bs=bst: (
                    lambda e: e.matmul(bt[0:1, 0:512], ones_col, bs[:, kk, :],
                                       start=(kk == 0), stop=(kk == NKK - 1))),
                    NKK, bank_buf, [bFLXB], extra_waits=[m_consts])
            mcol = stats_f[0:1, 0:512]
            vcol = stats_f[0:1, 512:1024]
            rcol = mrb_sb[0:1, 0:512]
            P.write("ve", lambda e: e.tensor_scalar_mul(mcol, pb[0][0:1, 0:512], 1.0 / E),
                    bufs_w=[bSTF], bufs_r=[bPB[0]], inc=c_ve)
            P.write("ve", lambda e: e.tensor_mul(vcol, mcol, mcol),
                    bufs_w=[bSTF], inc=c_ve)
            P.write("ve", lambda e: e.scalar_tensor_tensor(
                vcol, pb[1][0:1, 0:512], 1.0 / E, vcol, ALU.mult, ALU.subtract),
                    bufs_w=[bSTF], bufs_r=[bPB[1]], inc=c_ve)
            P.write("ac", lambda e: e.activation(vcol, vcol, AF.Sqrt, bias=eps_col),
                    bufs_w=[bSTF], waits=[m_consts], inc=c_ac)
            P.write("ve", lambda e: e.reciprocal(rcol, vcol),
                    bufs_w=[bMRB], bufs_r=[bSTF], inc=c_ve)
            P.write("ve", lambda e: e.tensor_mul(mcol, mcol, rcol),
                    bufs_w=[bSTF], bufs_r=[bMRB], inc=c_ve)
            P.write("ve", lambda e: e.tensor_copy(stats_b[0:1, 0:512], rcol),
                    bufs_w=[bSTB], bufs_r=[bMRB], inc=c_ve)
            P.write("ve", lambda e: e.tensor_copy(vcol, stats_b[0:1, 0:512]),
                    bufs_w=[bSTF], inc=c_ve)
            P.write("ve", lambda e: e.tensor_sub(stats_b[0:1, 512:1024], rcol, vcol),
                    bufs_w=[bSTB], bufs_r=[bMRB, bSTF], inc=c_ve)
            P.write("ve", lambda e: e.tensor_copy(stats_b[0:1, 1024:1536], mcol),
                    bufs_w=[bSTB], bufs_r=[bSTF], inc=c_ve)
            P.write("ve", lambda e: e.tensor_copy(vcol, stats_b[0:1, 1024:1536]),
                    bufs_w=[bSTF], inc=c_ve)
            P.write("ve", lambda e: e.tensor_sub(stats_b[0:1, 1536:2048], mcol, vcol),
                    bufs_w=[bSTB], bufs_r=[bSTF], inc=c_ve)
            accum_mms(lambda i: (
                lambda e, i=i: e.matmul(pb[0][:, 0:512], ones_row,
                                        stats_b[0:1, 1024 + i * 512:1536 + i * 512],
                                        start=(i == 0), stop=(i == 1))),
                2, bPB[0], [bSTB])
            accum_mms(lambda i: (
                lambda e, i=i: e.matmul(pb[1][:, 0:512], ones_row,
                                        stats_b[0:1, i * 512:512 + i * 512],
                                        start=(i == 0), stop=(i == 1))),
                2, bPB[1], [bSTB])
            P.write("ac", lambda e: e.activation(mrb_sb[:, :], pb[0][:, 0:512],
                                                 AF.Copy),
                    bufs_w=[bMRB], bufs_r=[bPB[0]], inc=c_ac)
            P.write("ve", lambda e: e.tensor_copy(rb_sb[:, :], pb[1][:, 0:512]),
                    bufs_w=[bRB], bufs_r=[bPB[1]], inc=c_ve)
            P.write("sy", lambda e, t0=t0: e.dma_start(
                x2g2[:, :, :], x2scr[:, t0:t0 + GRP].rearrange("(c p) t -> p c t", p=128)),
                    bufs_w=[bFLXB], waits=[x2st_marks[t0 // GRP]], inc=c_sy)
            for cch in range(8):
                P.write("ve", lambda e, c=cch: e.tensor_mul(
                    tmp[0][:, :], fsb[:, c, :], rb_sb[:, :]),
                        bufs_w=[bTMP[0]], bufs_r=[bH3, bRB], inc=c_ve)
                P.write("ve", lambda e, c=cch: e.tensor_sub(
                    tmp[1][:, :], tmp[0][:, :], mrb_sb[:, :]),
                        bufs_w=[bTMP[1]], bufs_r=[bTMP[0], bMRB], inc=c_ve)
                P.write("ve", lambda e, c=cch: e.tensor_add(
                    x2g2[:, c, :], x2g2[:, c, :], tmp[1][:, :]),
                        bufs_w=[bFLXB], bufs_r=[bTMP[1]], inc=c_ve)
            P.read("sy", lambda e, t0=t0: e.dma_start(
                outT[:, t0:t0 + GRP].rearrange("(c p) t -> p c t", p=128),
                x2g2[:, :, :]),
                   bufs_r=[bFLXB], inc=c_sy)

    final_out = (c_sy, c_sy.v)

    with (
        nc.semaphore("s_pe") as s_pe,
        nc.semaphore("s_ve") as s_ve,
        nc.semaphore("s_ac") as s_ac,
        nc.semaphore("s_gp") as s_gp,
        nc.semaphore("s_sy") as s_sy,
        nc.Block() as block,
    ):
        c_pe.sem, c_ve.sem, c_ac.sem, c_gp.sem, c_sy.sem = s_pe, s_ve, s_ac, s_gp, s_sy

        def runner(name, eng):
            lastw = {}
            for (en, waits, fn, inc) in P.steps:
                if en != name:
                    continue
                for (ctr, v) in waits:
                    if ctr.owner == name and not ctr.is_dma:
                        continue
                    key = id(ctr)
                    if lastw.get(key, 0) >= v:
                        continue
                    eng.wait_ge(ctr.sem, v)
                    lastw[key] = v
                if fn is None:
                    continue
                ins = fn(eng)
                if inc is not None and ins is not None:
                    ins.then_inc(inc.sem, inc.step)

        @block.tensor
        def _(t):
            runner("pe", t)

        @block.vector
        def _(v):
            runner("ve", v)

        @block.scalar
        def _(s):
            runner("ac", s)

        @block.gpsimd
        def _(g):
            runner("gp", g)

        @block.sync
        def _(s):
            runner("sy", s)
            s.wait_ge(c_sy.sem, final_out[1])

    return nc


def kernel(L0_attn_x, L0_attn_W_qkv, L0_attn_W_out, L0_ffn_w1_w, L0_ffn_w1_s,
           L0_ffn_w2_w, L0_ffn_w2_s, L0_ffn_h1_w, L0_ffn_h1_s, L0_ffn_h2_w,
           L0_ffn_h2_s, L0_ffn_w3_w, L0_ffn_w3_s):
    x = np.asarray(L0_attn_x)
    Bz, seq, _ = x.shape
    xT = np.ascontiguousarray(np.transpose(x, (0, 2, 1)).astype(np.float32))

    def wt(w, nk):
        return np.ascontiguousarray(np.asarray(w, dtype=np.float32).T.reshape(nk, 128, -1))

    in_common = {
        "wqkv_t": wt(L0_attn_W_qkv, NKK),
        "wout_t": wt(L0_attn_W_out, NKK),
        "w1_t": wt(L0_ffn_w1_w, NKK),
        "w2_t": wt(L0_ffn_w2_w, NKK),
        "h1_t": wt(L0_ffn_h1_w, NOC),
        "h2_t": wt(L0_ffn_h2_w, NOC),
        "w3_t": wt(L0_ffn_w3_w, NOC),
    }
    s_all = np.zeros((128, 136), dtype=np.float32)
    s_all[:, 0:32] = np.asarray(L0_ffn_w1_s, dtype=np.float32).reshape(32, 128).T
    s_all[:, 32:64] = np.asarray(L0_ffn_w2_s, dtype=np.float32).reshape(32, 128).T
    s_all[:, 64:96] = np.asarray(L0_ffn_h1_s, dtype=np.float32).reshape(32, 128).T
    s_all[:, 96:128] = np.asarray(L0_ffn_h2_s, dtype=np.float32).reshape(32, 128).T
    s_all[:, 128:136] = np.asarray(L0_ffn_w3_s, dtype=np.float32).reshape(8, 128).T
    in_common["s_all"] = s_all

    nc = build_program(seq)
    in_maps = [dict(in_common, xT=xT[b]) for b in range(Bz)]
    res = run_bass_kernel_spmd(nc, in_maps, core_ids=list(range(Bz)))
    out = np.stack([np.transpose(r["outT"], (1, 0)) for r in res.results])
    return np.ascontiguousarray(out.astype(np.float32))
